# version 15
# baseline (speedup 1.0000x reference)
import math
import numpy as np
import ml_dtypes

import concourse.bass as bass
import concourse.mybir as mybir
from concourse.bass_utils import run_bass_kernel_spmd

F32 = mybir.dt.float32
BF16 = mybir.dt.bfloat16
U8 = mybir.dt.uint8
AF = mybir.ActivationFunctionType
ALU = mybir.AluOpType
AX = mybir.AxisListType.X

D = 2048
II = 5632
T = 1024
ND = D // 128
NI = II // 128
NTT = T // 128
TH = 512
DENOM = float(np.sqrt(np.float32(2048.0)) / 4096.0)
EPS = 1e-5
INV_D = 1.0 / 2048.0

ENGS = ("sp", "gp", "pe", "act", "dve")


class Prog:

    ENG_SEM = {"dve": "sDVE", "act": "sACT"}

    def __init__(self):
        from collections import defaultdict
        self.ops = {e: [] for e in ENGS}
        self.cnt = defaultdict(int)
        self.last_wait = {}
        self.enabled = True
        self.meta = []
        self.wlog = {e: defaultdict(list) for e in self.ENG_SEM}
        self.rlog = {e: defaultdict(list) for e in self.ENG_SEM}

    def emit(self, eng, fn, waits=(), inc=None, inc_n=1, rd=(), wr=()):
        waits = list(waits)
        if self.enabled and eng in self.ENG_SEM:
            sem = self.ENG_SEM[eng]
            need = 0
            for (buf, lo, hi) in rd:
                for (l2, h2, c2) in self.wlog[eng][buf]:
                    if l2 < hi and lo < h2 and c2 > need:
                        need = c2
            for (buf, lo, hi) in wr:
                for log in (self.wlog[eng][buf], self.rlog[eng][buf]):
                    for (l2, h2, c2) in log:
                        if l2 < hi and lo < h2 and c2 > need:
                            need = c2
            if need > 0:
                waits.append((sem, need))
        ws = []
        for (sem, val) in waits:
            if val is None or val <= 0:
                continue
            k = (eng, sem)
            if self.last_wait.get(k, -1) >= val:
                continue
            self.last_wait[k] = val
            ws.append((sem, val))
        if not self.enabled:
            return self.cnt[inc] if inc is not None else None
        if inc is not None:
            self.cnt[inc] += inc_n
        self.ops[eng].append((ws, fn, inc, inc_n))
        self.meta.append((eng, len(self.ops[eng]) - 1, tuple(ws),
                          (inc, inc_n if inc else 0), tuple(rd), tuple(wr)))
        if eng in self.ENG_SEM and inc is not None:
            c = self.cnt[inc]
            for (buf, lo, hi) in wr:
                self.wlog[eng][buf].append((lo, hi, c))
            for (buf, lo, hi) in rd:
                self.rlog[eng][buf].append((lo, hi, c))
        return self.cnt[inc] if inc is not None else None


class Banks:

    def __init__(self, tensors):
        self.t = tensors
        self.pend = [[] for _ in tensors]
        self.i = 0

    def acquire(self):
        idx = self.i % len(self.t)
        self.i += 1
        w = self.pend[idx]
        self.pend[idx] = []
        return idx, self.t[idx], w

    def release(self, idx, sem, val):
        self.pend[idx].append((sem, val))


def build_nc(ph=("p0", "p1", "cc", "scan", "p2", "p3"), debug=False):
    nc = bass.Bass(trn_type="TRN2", target_bir_lowering=False)

    x_tm = nc.dram_tensor("x_tm", [T, D], F32, kind="ExternalInput").ap()
    x_fm = nc.dram_tensor("x_fm", [D, T], F32, kind="ExternalInput").ap()
    wq_b = nc.dram_tensor("wq_b", [ND, 128, D], F32, kind="ExternalInput").ap()
    wk_b = nc.dram_tensor("wk_b", [ND, 128, D], F32, kind="ExternalInput").ap()
    wv_b = nc.dram_tensor("wv_b", [ND, 128, D], F32, kind="ExternalInput").ap()
    wo_b = nc.dram_tensor("wo_b", [ND, 128, D], F32, kind="ExternalInput").ap()
    gate_b = nc.dram_tensor("gate_b", [NI // 2, 128, 2 * D], F32, kind="ExternalInput").ap()
    up_b = nc.dram_tensor("up_b", [NI // 2, 128, 2 * D], F32, kind="ExternalInput").ap()
    down_b = nc.dram_tensor("down_b", [ND, 128, II], F32, kind="ExternalInput").ap()
    gs_pm = nc.dram_tensor("gs_pm", [128, NI], F32, kind="ExternalInput").ap()
    us_pm = nc.dram_tensor("us_pm", [128, NI], F32, kind="ExternalInput").ap()
    ds_pm = nc.dram_tensor("ds_pm", [128, ND], F32, kind="ExternalInput").ap()
    mask_pm = nc.dram_tensor("mask_pm", [128, 8], F32, kind="ExternalInput").ap()
    ident_d = nc.dram_tensor("ident", [128, 128], BF16, kind="ExternalInput").ap()
    eps_d = nc.dram_tensor("eps_c", [128, 1], F32, kind="ExternalInput").ap()
    y = nc.dram_tensor("y", [D, T], F32, kind="ExternalOutput").ap()

    cc_in_h = nc.dram_tensor("cc_in", [1, 2048], F32)
    cc_out_h = nc.dram_tensor("cc_out", [8, 2048], F32)
    cc_in = cc_in_h.ap()
    cc_out = cc_out_h.ap()

    xn_fm = nc.alloc_sbuf_tensor("xn_fm", [128, ND * T], BF16).ap()
    vhat = nc.alloc_sbuf_tensor("vhat", [128, ND * T], BF16).ap()
    ident_s = nc.alloc_sbuf_tensor("ident_s", [128, 128], BF16).ap()
    gs_s = nc.alloc_sbuf_tensor("gs_s", [128, NI], F32).ap()
    us_s = nc.alloc_sbuf_tensor("us_s", [128, NI], F32).ap()
    ds_s = nc.alloc_sbuf_tensor("ds_s", [128, ND], F32).ap()
    mask_s = nc.alloc_sbuf_tensor("mask_s", [128, 8], F32).ap()
    totals = nc.alloc_sbuf_tensor("totals", [128, 16], F32).ap()
    gath = nc.alloc_sbuf_tensor("gath", [128, 128], F32).ap()
    off = nc.alloc_sbuf_tensor("off", [128, 16], F32).ap()
    kvt = nc.alloc_sbuf_tensor("kvt", [128, T], F32).ap()
    eps_s = nc.alloc_sbuf_tensor("eps_s", [128, 1], F32).ap()
    kbf = [nc.alloc_sbuf_tensor(f"kbf{i}", [128, TH], BF16).ap() for i in range(2)]
    bnst = [nc.alloc_sbuf_tensor(f"bnst{i}", [128, 24], F32).ap() for i in range(2)]
    bnag = [nc.alloc_sbuf_tensor(f"bnag{i}", [128, 2], F32).ap() for i in range(2)]
    st_sum = [nc.alloc_sbuf_tensor(f"st_sum{i}", [128, 1], F32).ap() for i in range(2)]
    st_sq = [nc.alloc_sbuf_tensor(f"st_sq{i}", [128, 1], F32).ap() for i in range(2)]
    st_mean = [nc.alloc_sbuf_tensor(f"st_mean{i}", [128, 1], F32).ap() for i in range(2)]
    st_m2 = [nc.alloc_sbuf_tensor(f"st_m2{i}", [128, 1], F32).ap() for i in range(2)]
    st_var = [nc.alloc_sbuf_tensor(f"st_var{i}", [128, 1], F32).ap() for i in range(2)]
    st_sd = [nc.alloc_sbuf_tensor(f"st_sd{i}", [128, 1], F32).ap() for i in range(2)]
    st_rstd = [nc.alloc_sbuf_tensor(f"st_rstd{i}", [128, 1], F32).ap() for i in range(2)]
    negmr = [nc.alloc_sbuf_tensor(f"negmr{i}", [128, 1], F32).ap() for i in range(2)]

    ARENA = 135168
    arena = nc.alloc_sbuf_tensor("arena", [128, ARENA], U8).ap()

    def carve(off_b, dt, n):
        sz = {F32: 4, BF16: 2}[dt]
        return arena[:, off_b:off_b + n * sz].bitcast(dt)

    xt = [carve(0, F32, D), carve(8192, F32, D)]
    sqtmp = carve(16384, F32, D)
    xn_tm = [carve(24576, BF16, D), carve(28672, BF16, D)]
    q_sl = carve(0, BF16, ND * T)
    xf = [carve(0, F32, T), carve(4096, F32, T)]
    out_sb = [carve(8192, F32, T), carve(12288, F32, T)]
    tmp_sb = [carve(16384, F32, TH), carve(18432, F32, TH)]
    p_sl = carve(32768, BF16, ND * T)
    gffn = carve(32768, BF16, ND * T)
    qkv_ring = [carve(65536 + i * 4096, BF16, D) for i in range(4)]
    hdn = carve(65536, BF16, NI * TH)
    GUD = [110592, 118784, 126976]
    GUA = [81920, 90112, 98304]
    gu_slot = [carve(o, BF16, 2 * D) for o in GUD + GUA]
    dn_off = [0, 11264]
    dn_ring = [carve(o, BF16, II) for o in dn_off]
    wo_ring = [carve(110592 + i * 4096, BF16, D) for i in range(2)]

    psf = [nc.alloc_psum_tensor(f"psf{i}", [128, TH], F32).ap() for i in range(6)]
    pst = [nc.alloc_psum_tensor(f"pst{i}", [128, 512], BF16).ap() for i in range(2)]

    P = Prog()
    fb = Banks(psf)
    tb = Banks(pst)

    xl0 = P.emit(
        "sp", lambda e, a=xt[0], b=x_tm[0:128, :]: e.dma_start(out=a, in_=b),
        inc="dX0", inc_n=16, wr=[("A", 0, 8192)])
    for dst, src, nm, sz in ((ident_s, ident_d, "ident_s", 256),
                             (gs_s, gs_pm, "gs_s", 176), (us_s, us_pm, "us_s", 176),
                             (ds_s, ds_pm, "ds_s", 64), (mask_s, mask_pm, "mask_s", 32),
                             (eps_s, eps_d, "eps_s", 4)):
        P.emit("sp", lambda e, a=dst, b=src: e.dma_start(out=a, in_=b),
               inc="dC", inc_n=16, wr=[(nm, 0, sz)])

    P.enabled = "p0" in ph
    import os
    _p0n = int(os.environ.get("P0_TILES", "8"))
    xload = [0] * NTT
    xn_tile_ready = [0] * NTT
    xn_done = [0] * NTT
    tr_done = {}
    last_xn_read = [0] * NTT

    for j in range(NTT):
        if j >= _p0n:
            P.enabled = False
        par = j % 2
        w = []
        if j >= 2:
            w.append(("sDVE", xn_done[j - 2]))
        if j == 0:
            xload[0] = xl0
        else:
            xload[j] = P.emit(
                "sp", lambda e, a=xt[par], b=x_tm[j * 128:(j + 1) * 128, :]:
                e.dma_start(out=a, in_=b), waits=w, inc=f"dX{par}", inc_n=16,
                wr=[("A", par * 8192, par * 8192 + 8192)])

        bs_done = 0
        for bq in range(4):
            bs_done = P.emit(
                "dve", lambda e, a=bnst[par][:, bq * 6:(bq + 1) * 6],
                b=xt[par][:, bq * 512:(bq + 1) * 512]:
                e.bn_stats(a, b),
                waits=[(f"dX{par}", xload[j])], inc="sDVE",
                rd=[("A", par * 8192 + bq * 2048, par * 8192 + bq * 2048 + 2048)],
                wr=[(f"bnst{par}", bq * 24, bq * 24 + 24)])
        ag_done = P.emit(
            "dve", lambda e, a=bnag[par], b=bnst[par]: e.bn_aggr(a, b),
            waits=[("sDVE", bs_done)], inc="sDVE",
            rd=[(f"bnst{par}", 0, 96)], wr=[(f"bnag{par}", 0, 8)])
        sd_done = P.emit(
            "act", lambda e, a=st_sd[par], b=bnag[par][:, 1:2], ep=eps_s:
            e.activation(out=a, in_=b, func=AF.Sqrt, bias=ep),
            waits=[("sDVE", ag_done), (f"dX{par}", xload[j]), ("dC", 96)],
            inc="sACT",
            rd=[(f"bnag{par}", 4, 8), ("eps_s", 0, 4)],
            wr=[(f"st_sd{par}", 0, 4)])
        rstd_done = P.emit("dve", lambda e, a=st_rstd[par], b=st_sd[par]:
               e.reciprocal(a, b), waits=[("sACT", sd_done)], inc="sDVE",
               rd=[(f"st_sd{par}", 0, 4)], wr=[(f"st_rstd{par}", 0, 4)])
        w = [("sPE", last_xn_read[j - 2])] if j >= 2 else []
        xn_done[j] = P.emit(
            "dve", lambda e, a=xn_tm[par], b=xt[par], m=bnag[par][:, 0:1],
            r=st_rstd[par]:
            e.tensor_scalar(out=a, in0=b, scalar1=m, scalar2=r,
                            op0=ALU.subtract, op1=ALU.mult),
            waits=w + [("sDVE", rstd_done)], inc="sDVE",
            rd=[("A", par * 8192, par * 8192 + 8192),
                (f"bnag{par}", 0, 4), (f"st_rstd{par}", 0, 4)],
            wr=[("A", 24576 + par * 4096, 24576 + par * 4096 + 4096)])

        for g in range(4):
            bi, bt, bw = tb.acquire()
            for cc in range(4):
                c = 4 * g + cc
                last = cc == 3
                tr_done[(j, g)] = P.emit(
                    "pe",
                    lambda e, o=bt[:, cc * 128:(cc + 1) * 128],
                    i=xn_tm[par][:, c * 128:(c + 1) * 128], idn=ident_s:
                    e.transpose(o, in_=i, identity=idn),
                    waits=([("sDVE", xn_done[j]), ("dC", 96)] + bw) if cc == 0 else [],
                    inc="sPE" if last else None,
                    rd=[("A", 24576 + par * 4096 + c * 256,
                         24576 + par * 4096 + c * 256 + 256), ("ident_s", 0, 256)],
                    wr=[(f"pst{(4 * j + g) % 2}", cc * 256, cc * 256 + 256)])
            ov = xn_fm.rearrange("p (c t) -> p c t", c=ND)[:, 4 * g:4 * g + 4,
                                                          j * 128:(j + 1) * 128]
            iv = bt.rearrange("p (c t) -> p c t", c=4)
            ev = P.emit(
                "act", lambda e, a=ov, b=iv: e.copy(out=a, in_=b),
                waits=[("sPE", tr_done[(j, g)])], inc="sACT",
                rd=[(f"pst{(4 * j + g) % 2}", 0, 1024)],
                wr=[("xn_fm", cx * 2048 + j * 256, cx * 2048 + j * 256 + 256)
                    for cx in range(4 * g, 4 * g + 4)])
            tb.release(bi, "sACT", ev)
        last_xn_read[j] = tr_done[(j, 3)]
        xn_tile_ready[j] = P.cnt["sACT"]
    XN_READY = P.cnt["sACT"]
    XN_READY_D = P.cnt["sDVE"]

    P.enabled = "p1" in ph
    ring_free = [[] for _ in range(4)]
    kbf_free = [[], [], [], []]
    qev = [0] * ND
    qev_last = 0

    def qkv_slab(P, si, wb, ec, nq=2):
        slot = si % 4
        span = T // nq
        lw = list(ring_free[slot])
        ring_free[slot] = []
        ld = P.emit(
            "gp", lambda e, a=qkv_ring[slot], b=wb[ec, :, :]:
            e.dma_start(out=a, in_=b), waits=lw, inc=f"dQ{slot}", inc_n=16,
            wr=[("A", 65536 + slot * 4096, 65536 + slot * 4096 + 4096)])
        gdone = [0] * nq
        binfo = []
        for qt in range(nq):
            bi, bt2, bw = fb.acquire()
            last_tile = (qt + 1) * (NTT // nq) - 1
            for kc in range(ND):
                gdone[qt] = P.emit(
                    "pe",
                    lambda e, o=bt2[:, 0:span], l=qkv_ring[slot][:, kc * 128:(kc + 1) * 128],
                    r=xn_fm[:, kc * T + qt * span:kc * T + (qt + 1) * span],
                    st=(kc == 0), sp=(kc == ND - 1):
                    e.matmul(o, lhsT=l, rhs=r, start=st, stop=sp),
                    waits=([(f"dQ{slot}", ld), ("sACT", xn_tile_ready[last_tile]),
                            ("sDVE", XN_READY_D if nq == 2 else 0)] + bw) if kc == 0 else [],
                    inc="sPE" if kc == ND - 1 else None,
                    rd=[("A", 65536 + slot * 4096 + kc * 256,
                         65536 + slot * 4096 + kc * 256 + 256),
                        ("xn_fm", kc * 2048 + qt * span, kc * 2048 + (qt + 1) * span)],
                    wr=[(f"psf{bi}", 0, span * 4)])
            binfo.append((bi, bt2))
        ring_free[slot].append(("sPE", gdone[nq - 1]))
        return gdone, binfo

    NQ = 4
    QS = T // NQ
    for ec in range(ND):
        gdone, binfo = qkv_slab(P, ec * 2, wk_b, ec, nq=NQ)
        kevs = [0] * NQ
        for qt in range(NQ):
            bi, bt2 = binfo[qt]
            kb, ko = qt // 2, (qt % 2) * QS
            kevs[qt] = P.emit(
                "act", lambda e, a=kbf[kb][:, ko:ko + QS], b=bt2[:, 0:QS]:
                e.copy(out=a, in_=b),
                waits=[("sPE", gdone[qt])] + kbf_free[qt], inc="sACT",
                rd=[(f"psf{bi}", 0, QS * 4)], wr=[(f"kbf{kb}", ko * 2, ko * 2 + QS * 2)])
            kbf_free[qt] = []
            fb.release(bi, "sACT", kevs[qt])
        gdone, binfo = qkv_slab(P, ec * 2 + 1, wv_b, ec, nq=NQ)
        for qt in range(NQ):
            bi, bt2 = binfo[qt]
            kb, ko = qt // 2, (qt % 2) * QS
            dst = ec * T + qt * QS
            pd = P.emit(
                "dve", lambda e, a=p_sl[:, dst:dst + QS], b=kbf[kb][:, ko:ko + QS],
                c=bt2[:, 0:QS]:
                e.tensor_mul(a, b, c),
                waits=[("sPE", gdone[qt]), ("sACT", kevs[qt])], inc="sDVE",
                rd=[(f"kbf{kb}", ko * 2, ko * 2 + QS * 2), (f"psf{bi}", 0, QS * 4)],
                wr=[("A", 32768 + dst * 2, 32768 + dst * 2 + QS * 2)])
            kbf_free[qt].append(("sDVE", pd))
            fb.release(bi, "sDVE", pd)
        P.emit("dve", lambda e, a=totals[:, ec:ec + 1], b=p_sl[:, ec * T:(ec + 1) * T]:
               e.reduce_sum(a, b, axis=AX), inc="sDVE",
               rd=[("A", 32768 + ec * 2048, 32768 + ec * 2048 + 2048)],
               wr=[("totals", ec * 4, ec * 4 + 4)])
    TOTALS_DONE = P.cnt["sDVE"]

    P.enabled = ("cc" in ph) or ("cc1" in ph)
    import os as _osg
    _NPRE = int(_osg.environ.get("NPRE", "6"))
    def gu_slot_of(H, s):
        return s if (H == 0 and s < _NPRE) else (s % 3)

    def gu_slot_bytes(slot):
        base = (GUD + GUA)[slot]
        return ("A", base, base + 8192)

    gu_pre = {}

    def emit_gu_pre():
        for s in range(_NPRE):
            wb2 = gate_b if s % 2 == 0 else up_b
            slot = s
            gu_pre[s] = P.emit(
                "gp", lambda e, a=gu_slot[slot], b=wb2[s // 2, :, :]:
                e.dma_start(out=a, in_=b), inc=f"dG{slot}", inc_n=16,
                wr=[gu_slot_bytes(slot)])

    P.enabled = "p1" in ph
    def q_chunk(ec):
        gdone, binfo = qkv_slab(P, 32 + ec, wq_b, ec)
        for h in range(2):
            bi, bt2 = binfo[h]
            dst = ec * T + h * TH
            qev[ec] = P.emit(
                "act", lambda e, a=q_sl[:, dst:dst + TH], b=bt2:
                e.copy(out=a, in_=b),
                waits=[("sPE", gdone[h])], inc="sACT",
                rd=[(f"psf{bi}", 0, 2048)],
                wr=[("A", dst * 2, dst * 2 + 1024)])
            fb.release(bi, "sACT", qev[ec])
    for ec in range(4):
        q_chunk(ec)
    qev_last = P.cnt["sACT"]

    P.enabled = ("cc" in ph) or ("cc2" in ph)
    ccin_ld = P.emit("sp", lambda e, a=cc_in.rearrange("o (p c) -> p o c", p=128),
           b=totals.rearrange("p (o c) -> p o c", o=1): e.dma_start(out=a, in_=b),
           waits=[("sDVE", TOTALS_DONE)], inc="dI", inc_n=16,
           rd=[("totals", 0, 64)], wr=[("Dcc_in", 0, 8192)])

    P.enabled = ("cc" in ph) or ("cc3" in ph)
    def do_cc(e):
        return e.collective_compute(
            "AllGather", ALU.bypass, replica_groups=[list(range(8))],
            ins=[cc_in.opt()], outs=[cc_out.opt()])
    P.emit("gp", do_cc, waits=[("dI", 16)], inc="sCC", inc_n=1,
           rd=[("Dcc_in", 0, 8192)], wr=[("Dcc_out", 0, 65536)])
    P.enabled = ("cc" in ph) or ("cc4" in ph)
    gath_ld = P.emit(
        "sp", lambda e, a=gath.rearrange("p (r c) -> p r c", r=8),
        b=cc_out.rearrange("r (p c) -> p r c", p=128):
        e.dma_start(out=a, in_=b), waits=[("sCC", 1)], inc="dH", inc_n=16,
        rd=[("Dcc_out", 0, 65536)], wr=[("gath", 0, 512)])

    P.enabled = "p1" in ph
    for ec in range(4, ND):
        q_chunk(ec)
    qev_last = P.cnt["sACT"]
    P.enabled = ("cc" in ph) or ("cc1" in ph)
    emit_gu_pre()

    P.enabled = ("cc" in ph) or ("cc5" in ph)
    off_done = P.emit("dve", lambda e, a=off, b=gath[:, 0:16], m=mask_s[:, 0:1]:
           e.tensor_scalar(out=a, in0=b, scalar1=m, scalar2=None, op0=ALU.mult),
           waits=[("dH", 16), ("dC", 96)], inc="sDVE",
           rd=[("gath", 0, 64), ("mask_s", 0, 4)], wr=[("off", 0, 64)])
    for r in range(1, 8):
        off_done = P.emit("dve",
               lambda e, a=off, b=gath[:, r * 16:(r + 1) * 16], m=mask_s[:, r:r + 1]:
               e.scalar_tensor_tensor(out=a, in0=b, scalar=m, in1=a,
                                      op0=ALU.mult, op1=ALU.add),
               waits=[("sDVE", off_done)], inc="sDVE",
               rd=[("gath", r * 64, r * 64 + 64), ("mask_s", r * 4, r * 4 + 4),
                   ("off", 0, 64)], wr=[("off", 0, 64)])

    P.enabled = "scan" in ph
    first_scan_w = [("sDVE", off_done)] if "cc" in ph else []
    for ec in range(ND):
        P.emit("dve",
               lambda e, a=kvt, b=p_sl[:, ec * T:(ec + 1) * T], i=off[:, ec:ec + 1]:
               e.tensor_tensor_scan(out=a, data0=b, data1=b, initial=i,
                                    op0=ALU.add, op1=ALU.bypass),
               waits=first_scan_w, inc="sDVE",
               rd=[("A", 32768 + ec * 2048, 32768 + ec * 2048 + 2048),
                   ("off", ec * 4, ec * 4 + 4)], wr=[("kvt", 0, 4096)])
        P.emit("dve",
               lambda e, a=vhat[:, ec * T:(ec + 1) * T], b=kvt,
               c=q_sl[:, ec * T:(ec + 1) * T]:
               e.scalar_tensor_tensor(out=a, in0=b, scalar=DENOM, in1=c,
                                      op0=ALU.mult, op1=ALU.mult),
               waits=[("sACT", qev[ec])], inc="sDVE",
               rd=[("kvt", 0, 4096), ("A", ec * 2048, ec * 2048 + 2048)],
               wr=[("vhat", ec * 2048, ec * 2048 + 2048)])
    VHAT_DONE = P.cnt["sDVE"]
    if debug:
        dbg = {}
        for nm, ap_, dt in (("d_q", q_sl, BF16), ("d_p", p_sl, BF16),
                            ("d_tot", totals, F32), ("d_off", off, F32),
                            ("d_gath", gath, F32)):
            dd = nc.dram_tensor(nm, [128, ap_.shape[1]], dt, kind="ExternalOutput").ap()
            P.emit("sp", lambda e, a=dd, b=ap_: e.dma_start(out=a, in_=b),
                   waits=[("sDVE", VHAT_DONE)], inc="dB", inc_n=16)

    P.enabled = "p2" in ph
    gate_ev = {}
    up_ev = {}
    down_ev = {}
    slot_free = [[] for _ in range(6)]
    dn_free = [[], []]
    first_down_evac = True
    down_pass_pe = [0, 0]
    gu_pass_pe = [0, 0]
    a_slabs_done = 0
    for H in range(2):
        for s in range(2 * (NI // 2)):
            isl = s // 2
            is_up = s % 2 == 1
            slot = gu_slot_of(H, s)
            if H == 0 and s < _NPRE:
                ld = gu_pre[s]
            else:
                lw = list(slot_free[slot])
                slot_free[slot] = []
                wb2 = up_b if is_up else gate_b
                ld = P.emit("gp", lambda e, a=gu_slot[slot], b=wb2[isl, :, :]:
                            e.dma_start(out=a, in_=b), waits=lw,
                            inc=f"dG{slot}", inc_n=16, wr=[gu_slot_bytes(slot)])
            last_mm = 0
            for jj in range(2):
                ic = 2 * isl + jj
                bi, bt2, bw = fb.acquire()
                gg = 0
                for kc in range(ND):
                    gg = P.emit(
                        "pe",
                        lambda e, o=bt2,
                        l=gu_slot[slot][:, kc * 256 + jj * 128:kc * 256 + jj * 128 + 128],
                        r=xn_fm[:, kc * T + H * TH:kc * T + H * TH + TH],
                        st=(kc == 0), sp=(kc == ND - 1):
                        e.matmul(o, lhsT=l, rhs=r, start=st, stop=sp),
                        waits=([(f"dG{slot}", ld)] + bw) if kc == 0 else [],
                        inc="sPE" if kc == ND - 1 else None,
                        rd=[(gu_slot_bytes(slot)[0],
                             gu_slot_bytes(slot)[1] + kc * 512 + jj * 256,
                             gu_slot_bytes(slot)[1] + kc * 512 + jj * 256 + 256),
                            ("xn_fm", kc * 2048 + H * 1024, kc * 2048 + H * 1024 + 1024)],
                        wr=[(f"psf{bi}", 0, 2048)])
                last_mm = gg
                if not is_up:
                    gate_ev[(H, ic)] = P.emit(
                        "act", lambda e, a=hdn[:, ic * TH:(ic + 1) * TH], b=bt2,
                        s2=gs_s[:, ic:ic + 1]:
                        e.activation(out=a, in_=b, func=AF.Silu, scale=s2),
                        waits=[("sPE", gg), ("dC", 96)], inc="sACT",
                        rd=[(f"psf{bi}", 0, 2048), ("gs_s", ic * 4, ic * 4 + 4)],
                        wr=[("A", 65536 + ic * 1024, 65536 + ic * 1024 + 1024)])
                    fb.release(bi, "sACT", gate_ev[(H, ic)])
                else:
                    up_ev[(H, ic)] = P.emit(
                        "dve", lambda e, a=hdn[:, ic * TH:(ic + 1) * TH], b=bt2,
                        s2=us_s[:, ic:ic + 1]:
                        e.scalar_tensor_tensor(out=a, in0=b, scalar=s2, in1=a,
                                               op0=ALU.mult, op1=ALU.mult),
                        waits=[("sPE", gg), ("dC", 96),
                               ("sACT", gate_ev[(H, ic)])], inc="sDVE",
                        rd=[(f"psf{bi}", 0, 2048), ("us_s", ic * 4, ic * 4 + 4),
                            ("A", 65536 + ic * 1024, 65536 + ic * 1024 + 1024)],
                        wr=[("A", 65536 + ic * 1024, 65536 + ic * 1024 + 1024)])
                    fb.release(bi, "sDVE", up_ev[(H, ic)])
            slot_free[slot].append(("sPE", last_mm))
            if H == 0 and s == 5:
                a_slabs_done = last_mm
        HDN_ACT = gate_ev[(H, NI - 1)]
        HDN_DVE = up_ev[(H, NI - 1)]
        gu_pass_pe[H] = P.cnt["sPE"]
        for dc in range(ND):
            slot = dc % 2
            lw = list(dn_free[slot])
            dn_free[slot] = []
            if H == 0 and dc < 2:
                lw.append(("sDVE", VHAT_DONE))
            ld = P.emit("gp", lambda e, a=dn_ring[slot], b=down_b[dc, :, :]:
                        e.dma_start(out=a, in_=b), waits=lw, inc=f"dD{slot}", inc_n=16,
                        wr=[("A", dn_off[slot], dn_off[slot] + 11264)])
            bi, bt2, bw = fb.acquire()
            dg = 0
            for ic in range(NI):
                dg = P.emit(
                    "pe",
                    lambda e, o=bt2, l=dn_ring[slot][:, ic * 128:(ic + 1) * 128],
                    r=hdn[:, ic * TH:(ic + 1) * TH],
                    st=(ic == 0), sp=(ic == NI - 1):
                    e.matmul(o, lhsT=l, rhs=r, start=st, stop=sp),
                    waits=([(f"dD{slot}", ld), ("sACT", HDN_ACT),
                            ("sDVE", HDN_DVE)] + bw) if ic == 0 else [],
                    inc="sPE" if ic == NI - 1 else None,
                    rd=[("A", dn_off[slot] + ic * 256, dn_off[slot] + ic * 256 + 256),
                        ("A", 65536 + ic * 1024, 65536 + ic * 1024 + 1024)],
                    wr=[(f"psf{bi}", 0, 2048)])
            dn_free[slot].append(("sPE", dg))
            w = [("sPE", dg)]
            if first_down_evac:
                w.append(("sDVE", VHAT_DONE))
                first_down_evac = False
            down_ev[(H, dc)] = P.emit(
                "act", lambda e, a=gffn[:, dc * T + H * TH:dc * T + H * TH + TH],
                b=bt2, s2=ds_s[:, dc:dc + 1]:
                e.activation(out=a, in_=b, func=AF.Silu, scale=s2),
                waits=w + [("dC", 96)], inc="sACT",
                rd=[(f"psf{bi}", 0, 2048), ("ds_s", dc * 4, dc * 4 + 4)],
                wr=[("A", 32768 + (dc * T + H * TH) * 2,
                     32768 + (dc * T + H * TH) * 2 + 1024)])
            fb.release(bi, "sACT", down_ev[(H, dc)])
        down_pass_pe[H] = P.cnt["sPE"]

    P.enabled = "p3" in ph
    wo_free = [[], []]
    store_cnt = [0] * ND
    fin = {}
    for fc in range(ND):
        slot = fc % 2
        lw = list(wo_free[slot])
        wo_free[slot] = []
        lw.append(("sPE", gu_pass_pe[1]))
        ld = P.emit("gp", lambda e, a=wo_ring[slot], b=wo_b[fc, :, :]:
                    e.dma_start(out=a, in_=b), waits=lw, inc=f"dW{slot}", inc_n=16,
                    wr=[("A", 110592 + slot * 4096, 110592 + slot * 4096 + 4096)])
        par = fc % 2
        w = [("sDVE", VHAT_DONE), ("sPE", down_pass_pe[1])]
        if fc >= 2:
            w.append(("sDVE", fin[(fc - 2, 1)]))
        xfl = P.emit("sp", lambda e, a=xf[par], b=x_fm[fc * 128:(fc + 1) * 128, :]:
                     e.dma_start(out=a, in_=b), waits=w, inc=f"dF{par}", inc_n=16,
                     wr=[("A", par * 4096, par * 4096 + 4096)])
        for h in range(2):
            bi, bt2, bw = fb.acquire()
            wg = 0
            for ec in range(ND):
                wg = P.emit(
                    "pe",
                    lambda e, o=bt2, l=wo_ring[slot][:, ec * 128:(ec + 1) * 128],
                    r=vhat[:, ec * T + h * TH:ec * T + h * TH + TH],
                    st=(ec == 0), sp=(ec == ND - 1):
                    e.matmul(o, lhsT=l, rhs=r, start=st, stop=sp),
                    waits=([(f"dW{slot}", ld), ("sDVE", VHAT_DONE)] + bw) if ec == 0 else [],
                    inc="sPE" if ec == ND - 1 else None,
                    rd=[("A", 110592 + slot * 4096 + ec * 256,
                         110592 + slot * 4096 + ec * 256 + 256),
                        ("vhat", ec * 2048 + h * 1024, ec * 2048 + h * 1024 + 1024)],
                    wr=[(f"psf{bi}", 0, 2048)])
            wo_free[slot].append(("sPE", wg))
            P.emit("dve",
                   lambda e, a=tmp_sb[h], b=bt2,
                   g=gffn[:, fc * T + h * TH:fc * T + h * TH + TH]:
                   e.tensor_mul(a, b, g),
                   waits=[("sPE", wg), ("sACT", down_ev[(h, fc)])], inc="sDVE",
                   rd=[(f"psf{bi}", 0, 2048),
                       ("A", 32768 + (fc * T + h * TH) * 2,
                        32768 + (fc * T + h * TH) * 2 + 1024)],
                   wr=[("A", 16384 + h * 2048, 16384 + h * 2048 + 2048)])
            w2 = [(f"dF{par}", xfl)]
            if fc >= 2:
                w2.append((f"dS{par}", store_cnt[fc - 2]))
            fin[(fc, h)] = P.emit(
                "dve", lambda e, a=out_sb[par][:, h * TH:h * TH + TH], b=tmp_sb[h],
                c=xf[par][:, h * TH:h * TH + TH]:
                e.tensor_add(a, b, c), waits=w2, inc="sDVE",
                rd=[("A", 16384 + h * 2048, 16384 + h * 2048 + 2048),
                    ("A", par * 4096 + h * 2048, par * 4096 + h * 2048 + 2048)],
                wr=[("A", 8192 + par * 4096 + h * 2048,
                     8192 + par * 4096 + h * 2048 + 2048)])
            fb.release(bi, "sDVE", fin[(fc, h)])
        store_cnt[fc] = P.emit(
            "sp", lambda e, a=y[fc * 128:(fc + 1) * 128, :], b=out_sb[par]:
            e.dma_start(out=a, in_=b),
            waits=[("sDVE", fin[(fc, 1)])], inc=f"dS{par}", inc_n=16,
            rd=[("A", 8192 + par * 4096, 8192 + par * 4096 + 4096)],
            wr=[("Dy", fc * 4096, fc * 4096 + 4096)])

    P.enabled = True
    if "p3" not in ph:
        P.emit("sp", lambda e: e.dma_start(out=y[0:128, :], in_=out_sb[0]),
               inc="dS0", inc_n=16)
    if debug:
        for nm, ap_, dt, wsem, wval in (
                ("d_xn", xn_fm, BF16, "sACT", XN_READY),
                ("d_vhat", vhat, BF16, "sDVE", VHAT_DONE),
                ("d_gffn", gffn, BF16, "sACT", P.cnt["sACT"]),
                ("d_hdn", hdn, BF16, "sACT", P.cnt["sACT"])):
            dd = nc.dram_tensor(nm, [128, ap_.shape[1]], dt, kind="ExternalOutput").ap()
            P.emit("sp", lambda e, a=dd, b=ap_: e.dma_start(out=a, in_=b),
                   waits=[(wsem, wval)], inc="dB", inc_n=16)
    fw = [("dS0", P.cnt["dS0"]), ("dS1", P.cnt["dS1"])]
    if P.cnt["dB"]:
        fw.append(("dB", P.cnt["dB"]))
    P.emit("sp", None, waits=fw)

    nc._plan = P
    used = set()
    for e in ENGS:
        for ws, fn, inc, inc_n in P.ops[e]:
            used.update(s for s, v in ws)
            if inc:
                used.add(inc)
    from contextlib import ExitStack
    with ExitStack() as ctx:
        sems = {s: ctx.enter_context(nc.semaphore(s)) for s in sorted(used)}
        block = ctx.enter_context(nc.Block())

        def replay(key):
            def run(eng):
                for ws, fn, inc, inc_n in P.ops[key]:
                    for sem, val in ws:
                        eng.wait_ge(sems[sem], val)
                    if fn is None:
                        continue
                    inst = fn(eng)
                    if inc is not None:
                        inst.then_inc(sems[inc], inc_n)
            return run

        block.sync(replay("sp"))
        block.gpsimd(replay("gp"))
        block.tensor(replay("pe"))
        block.scalar(replay("act"))
        block.vector(replay("dve"))

    return nc


def _block_w(WT, K, M, mchunk=128):
    A = WT.reshape(K // 128, 128, M // mchunk, mchunk).transpose(2, 1, 0, 3)
    return np.ascontiguousarray(A.reshape(M // mchunk, 128, (K // 128) * mchunk))


_NC_CACHE = {}


def kernel(L0_attn_x, L0_attn_Wq, L0_attn_Wk, L0_attn_Wv, L0_attn_Wo,
           L0_ffn_gate_w, L0_ffn_up_w, L0_ffn_down_w,
           L0_ffn_gate_s, L0_ffn_up_s, L0_ffn_down_s):
    x = np.asarray(L0_attn_x, dtype=np.float32)
    Wq = np.asarray(L0_attn_Wq, dtype=np.float32)
    Wk = np.asarray(L0_attn_Wk, dtype=np.float32)
    Wv = np.asarray(L0_attn_Wv, dtype=np.float32)
    Wo = np.asarray(L0_attn_Wo, dtype=np.float32)
    gw = np.asarray(L0_ffn_gate_w, dtype=np.float32)
    uw = np.asarray(L0_ffn_up_w, dtype=np.float32)
    dw = np.asarray(L0_ffn_down_w, dtype=np.float32)
    gs = np.asarray(L0_ffn_gate_s, dtype=np.float32)
    us = np.asarray(L0_ffn_up_s, dtype=np.float32)
    ds = np.asarray(L0_ffn_down_s, dtype=np.float32)

    wq_b = _block_w(np.ascontiguousarray(Wq.T), D, D)
    wk_b = _block_w(np.ascontiguousarray(Wk.T), D, D)
    wv_b = _block_w(np.ascontiguousarray(Wv.T), D, D)
    wo_b = _block_w(np.ascontiguousarray(Wo.T), D, D)
    gate_b = _block_w(np.ascontiguousarray(gw.T), D, II, mchunk=256)
    up_b = _block_w(np.ascontiguousarray(uw.T), D, II, mchunk=256)
    down_b = _block_w(np.ascontiguousarray(dw.T), II, D)

    gs_pm = np.ascontiguousarray(gs.reshape(NI, 128).T)
    us_pm = np.ascontiguousarray(us.reshape(NI, 128).T)
    ds_pm = np.ascontiguousarray(ds.reshape(ND, 128).T)
    ident = np.eye(128, dtype=ml_dtypes.bfloat16)

    in_maps = []
    for c in range(8):
        b = c // 4
        t0 = (c % 4) * T
        xs = np.ascontiguousarray(x[b, t0:t0 + T, :])
        xfm = np.ascontiguousarray(xs.T)
        m8 = np.zeros(8, np.float32)
        m8[(c // 4) * 4:c] = 1.0
        mask = np.ascontiguousarray(np.tile(m8, (128, 1)))
        in_maps.append(dict(
            x_tm=xs, x_fm=xfm, wq_b=wq_b, wk_b=wk_b, wv_b=wv_b, wo_b=wo_b,
            gate_b=gate_b, up_b=up_b, down_b=down_b,
            gs_pm=gs_pm, us_pm=us_pm, ds_pm=ds_pm, mask_pm=mask, ident=ident,
            eps_c=np.full((128, 1), EPS, np.float32)))

    if "nc" not in _NC_CACHE:
        _NC_CACHE["nc"] = build_nc()
    nc = _NC_CACHE["nc"]
    res = run_bass_kernel_spmd(nc, in_maps, core_ids=list(range(8)))

    out = np.empty((2, 4096, D), np.float32)
    for c in range(8):
        b = c // 4
        t0 = (c % 4) * T
        out[b, t0:t0 + T, :] = np.asarray(res.results[c]["y"], np.float32).T
    return out


# version 17
# speedup vs baseline: 1.0241x; 1.0241x over previous
import math
import numpy as np
import ml_dtypes

import concourse.bass as bass
import concourse.mybir as mybir
from concourse.bass_utils import run_bass_kernel_spmd

F32 = mybir.dt.float32
BF16 = mybir.dt.bfloat16
U8 = mybir.dt.uint8
AF = mybir.ActivationFunctionType
ALU = mybir.AluOpType
AX = mybir.AxisListType.X

D = 2048
II = 5632
T = 1024
ND = D // 128
NI = II // 128
NTT = T // 128
TH = 512
DENOM = float(np.sqrt(np.float32(2048.0)) / 4096.0)
EPS = 1e-5
INV_D = 1.0 / 2048.0

ENGS = ("sp", "gp", "pe", "act", "dve")


class Prog:

    ENG_SEM = {"dve": "sDVE", "act": "sACT"}

    def __init__(self):
        from collections import defaultdict
        self.ops = {e: [] for e in ENGS}
        self.cnt = defaultdict(int)
        self.last_wait = {}
        self.enabled = True
        self.meta = []
        self.wlog = {e: defaultdict(list) for e in self.ENG_SEM}
        self.rlog = {e: defaultdict(list) for e in self.ENG_SEM}

    def emit(self, eng, fn, waits=(), inc=None, inc_n=1, rd=(), wr=()):
        waits = list(waits)
        if self.enabled and eng in self.ENG_SEM:
            sem = self.ENG_SEM[eng]
            need = 0
            for (buf, lo, hi) in rd:
                for (l2, h2, c2) in self.wlog[eng][buf]:
                    if l2 < hi and lo < h2 and c2 > need:
                        need = c2
            for (buf, lo, hi) in wr:
                for log in (self.wlog[eng][buf], self.rlog[eng][buf]):
                    for (l2, h2, c2) in log:
                        if l2 < hi and lo < h2 and c2 > need:
                            need = c2
            if need > 0:
                waits.append((sem, need))
        ws = []
        for (sem, val) in waits:
            if val is None or val <= 0:
                continue
            k = (eng, sem)
            if self.last_wait.get(k, -1) >= val:
                continue
            self.last_wait[k] = val
            ws.append((sem, val))
        if not self.enabled:
            return self.cnt[inc] if inc is not None else None
        if inc is not None:
            self.cnt[inc] += inc_n
        self.ops[eng].append((ws, fn, inc, inc_n))
        self.meta.append((eng, len(self.ops[eng]) - 1, tuple(ws),
                          (inc, inc_n if inc else 0), tuple(rd), tuple(wr)))
        if eng in self.ENG_SEM and inc is not None:
            c = self.cnt[inc]
            for (buf, lo, hi) in wr:
                self.wlog[eng][buf].append((lo, hi, c))
            for (buf, lo, hi) in rd:
                self.rlog[eng][buf].append((lo, hi, c))
        return self.cnt[inc] if inc is not None else None


class Banks:

    def __init__(self, tensors):
        self.t = tensors
        self.pend = [[] for _ in tensors]
        self.i = 0

    def acquire(self):
        idx = self.i % len(self.t)
        self.i += 1
        w = self.pend[idx]
        self.pend[idx] = []
        return idx, self.t[idx], w

    def release(self, idx, sem, val):
        self.pend[idx].append((sem, val))


def build_nc(ph=("p0", "p1", "cc", "scan", "p2", "p3"), debug=False):
    nc = bass.Bass(trn_type="TRN2", target_bir_lowering=False)

    x_tm = nc.dram_tensor("x_tm", [T, D], F32, kind="ExternalInput").ap()
    x_fm = nc.dram_tensor("x_fm", [D, T], F32, kind="ExternalInput").ap()
    wq_b = nc.dram_tensor("wq_b", [ND, 128, D], F32, kind="ExternalInput").ap()
    wk_b = nc.dram_tensor("wk_b", [ND, 128, D], F32, kind="ExternalInput").ap()
    wv_b = nc.dram_tensor("wv_b", [ND, 128, D], F32, kind="ExternalInput").ap()
    wo_b = nc.dram_tensor("wo_b", [ND, 128, D], F32, kind="ExternalInput").ap()
    gate_b = nc.dram_tensor("gate_b", [NI // 2, 128, 2 * D], F32, kind="ExternalInput").ap()
    up_b = nc.dram_tensor("up_b", [NI // 2, 128, 2 * D], F32, kind="ExternalInput").ap()
    down_b = nc.dram_tensor("down_b", [ND, 128, II], F32, kind="ExternalInput").ap()
    gs_pm = nc.dram_tensor("gs_pm", [128, NI], F32, kind="ExternalInput").ap()
    us_pm = nc.dram_tensor("us_pm", [128, NI], F32, kind="ExternalInput").ap()
    ds_pm = nc.dram_tensor("ds_pm", [128, ND], F32, kind="ExternalInput").ap()
    mask_pm = nc.dram_tensor("mask_pm", [128, 8], F32, kind="ExternalInput").ap()
    ident_d = nc.dram_tensor("ident", [128, 128], BF16, kind="ExternalInput").ap()
    eps_d = nc.dram_tensor("eps_c", [128, 1], F32, kind="ExternalInput").ap()
    y = nc.dram_tensor("y", [D, T], F32, kind="ExternalOutput").ap()

    cc_in_h = nc.dram_tensor("cc_in", [1, 2048], F32)
    cc_out_h = nc.dram_tensor("cc_out", [8, 2048], F32)
    cc_in = cc_in_h.ap()
    cc_out = cc_out_h.ap()

    xn_fm = nc.alloc_sbuf_tensor("xn_fm", [128, ND * T], BF16).ap()
    vhat = nc.alloc_sbuf_tensor("vhat", [128, ND * T], BF16).ap()
    ident_s = nc.alloc_sbuf_tensor("ident_s", [128, 128], BF16).ap()
    gs_s = nc.alloc_sbuf_tensor("gs_s", [128, NI], F32).ap()
    us_s = nc.alloc_sbuf_tensor("us_s", [128, NI], F32).ap()
    ds_s = nc.alloc_sbuf_tensor("ds_s", [128, ND], F32).ap()
    mask_s = nc.alloc_sbuf_tensor("mask_s", [128, 8], F32).ap()
    totals = nc.alloc_sbuf_tensor("totals", [128, 16], F32).ap()
    gath = nc.alloc_sbuf_tensor("gath", [128, 128], F32).ap()
    off = nc.alloc_sbuf_tensor("off", [128, 16], F32).ap()
    kvt = nc.alloc_sbuf_tensor("kvt", [128, T], F32).ap()
    eps_s = nc.alloc_sbuf_tensor("eps_s", [128, 1], F32).ap()
    kbf = [nc.alloc_sbuf_tensor(f"kbf{i}", [128, TH], BF16).ap() for i in range(2)]
    bnst = [nc.alloc_sbuf_tensor(f"bnst{i}", [128, 24], F32).ap() for i in range(2)]
    bnag = [nc.alloc_sbuf_tensor(f"bnag{i}", [128, 2], F32).ap() for i in range(2)]
    st_sum = [nc.alloc_sbuf_tensor(f"st_sum{i}", [128, 1], F32).ap() for i in range(2)]
    st_sq = [nc.alloc_sbuf_tensor(f"st_sq{i}", [128, 1], F32).ap() for i in range(2)]
    st_mean = [nc.alloc_sbuf_tensor(f"st_mean{i}", [128, 1], F32).ap() for i in range(2)]
    st_m2 = [nc.alloc_sbuf_tensor(f"st_m2{i}", [128, 1], F32).ap() for i in range(2)]
    st_var = [nc.alloc_sbuf_tensor(f"st_var{i}", [128, 1], F32).ap() for i in range(2)]
    st_sd = [nc.alloc_sbuf_tensor(f"st_sd{i}", [128, 1], F32).ap() for i in range(2)]
    st_rstd = [nc.alloc_sbuf_tensor(f"st_rstd{i}", [128, 1], F32).ap() for i in range(2)]
    negmr = [nc.alloc_sbuf_tensor(f"negmr{i}", [128, 1], F32).ap() for i in range(2)]

    ARENA = 135168
    arena = nc.alloc_sbuf_tensor("arena", [128, ARENA], U8).ap()

    def carve(off_b, dt, n):
        sz = {F32: 4, BF16: 2}[dt]
        return arena[:, off_b:off_b + n * sz].bitcast(dt)

    xt = [carve(0, F32, D), carve(8192, F32, D)]
    sqtmp = carve(16384, F32, D)
    xn_tm = [carve(24576, BF16, D), carve(28672, BF16, D)]
    q_sl = carve(0, BF16, ND * T)
    xf = [carve(0, F32, T), carve(4096, F32, T)]
    out_sb = [carve(8192, F32, T), carve(12288, F32, T)]
    tmp_sb = [carve(16384, F32, TH), carve(18432, F32, TH)]
    p_sl = carve(32768, BF16, ND * T)
    gffn = carve(32768, BF16, ND * T)
    qkv_ring = [carve(65536 + i * 4096, BF16, D) for i in range(4)]
    hdn = carve(65536, BF16, NI * TH)
    GUD = [110592, 118784, 126976]
    GUA = [81920, 90112, 98304]
    gu_slot = [carve(o, BF16, 2 * D) for o in GUD + GUA]
    dn_off = [0, 11264]
    dn_ring = [carve(o, BF16, II) for o in dn_off]
    wo_ring = [carve(110592 + i * 4096, BF16, D) for i in range(2)]

    psf = [nc.alloc_psum_tensor(f"psf{i}", [128, TH], F32).ap() for i in range(6)]
    pst = [nc.alloc_psum_tensor(f"pst{i}", [128, 512], BF16).ap() for i in range(2)]

    P = Prog()
    fb = Banks(psf)
    tb = Banks(pst)

    xl0 = P.emit(
        "sp", lambda e, a=xt[0], b=x_tm[0:128, :]: e.dma_start(out=a, in_=b),
        inc="dX0", inc_n=16, wr=[("A", 0, 8192)])
    for dst, src, nm, sz in ((ident_s, ident_d, "ident_s", 256),
                             (gs_s, gs_pm, "gs_s", 176), (us_s, us_pm, "us_s", 176),
                             (ds_s, ds_pm, "ds_s", 64), (mask_s, mask_pm, "mask_s", 32),
                             (eps_s, eps_d, "eps_s", 4)):
        P.emit("sp", lambda e, a=dst, b=src: e.dma_start(out=a, in_=b),
               inc="dC", inc_n=16, wr=[(nm, 0, sz)])

    P.enabled = "p0" in ph
    import os
    _p0n = int(os.environ.get("P0_TILES", "8"))
    xload = [0] * NTT
    xn_tile_ready = [0] * NTT
    xn_done = [0] * NTT
    tr_done = {}
    last_xn_read = [0] * NTT

    for j in range(NTT):
        if j >= _p0n:
            P.enabled = False
        par = j % 2
        w = []
        if j >= 2:
            w.append(("sDVE", xn_done[j - 2]))
        if j == 0:
            xload[0] = xl0
        else:
            xload[j] = P.emit(
                "sp", lambda e, a=xt[par], b=x_tm[j * 128:(j + 1) * 128, :]:
                e.dma_start(out=a, in_=b), waits=w, inc=f"dX{par}", inc_n=16,
                wr=[("A", par * 8192, par * 8192 + 8192)])

        bs_done = 0
        for bq in range(4):
            bs_done = P.emit(
                "dve", lambda e, a=bnst[par][:, bq * 6:(bq + 1) * 6],
                b=xt[par][:, bq * 512:(bq + 1) * 512]:
                e.bn_stats(a, b),
                waits=[(f"dX{par}", xload[j])], inc="sDVE",
                rd=[("A", par * 8192 + bq * 2048, par * 8192 + bq * 2048 + 2048)],
                wr=[(f"bnst{par}", bq * 24, bq * 24 + 24)])
        ag_done = P.emit(
            "dve", lambda e, a=bnag[par], b=bnst[par]: e.bn_aggr(a, b),
            waits=[("sDVE", bs_done)], inc="sDVE",
            rd=[(f"bnst{par}", 0, 96)], wr=[(f"bnag{par}", 0, 8)])
        sd_done = P.emit(
            "act", lambda e, a=st_sd[par], b=bnag[par][:, 1:2], ep=eps_s:
            e.activation(out=a, in_=b, func=AF.Sqrt, bias=ep),
            waits=[("sDVE", ag_done), (f"dX{par}", xload[j]), ("dC", 96)],
            inc="sACT",
            rd=[(f"bnag{par}", 4, 8), ("eps_s", 0, 4)],
            wr=[(f"st_sd{par}", 0, 4)])
        rstd_done = P.emit("dve", lambda e, a=st_rstd[par], b=st_sd[par]:
               e.reciprocal(a, b), waits=[("sACT", sd_done)], inc="sDVE",
               rd=[(f"st_sd{par}", 0, 4)], wr=[(f"st_rstd{par}", 0, 4)])
        w = [("sPE", last_xn_read[j - 2])] if j >= 2 else []
        xn_done[j] = P.emit(
            "dve", lambda e, a=xn_tm[par], b=xt[par], m=bnag[par][:, 0:1],
            r=st_rstd[par]:
            e.tensor_scalar(out=a, in0=b, scalar1=m, scalar2=r,
                            op0=ALU.subtract, op1=ALU.mult),
            waits=w + [("sDVE", rstd_done)], inc="sDVE",
            rd=[("A", par * 8192, par * 8192 + 8192),
                (f"bnag{par}", 0, 4), (f"st_rstd{par}", 0, 4)],
            wr=[("A", 24576 + par * 4096, 24576 + par * 4096 + 4096)])

        for g in range(4):
            bi, bt, bw = tb.acquire()
            for cc in range(4):
                c = 4 * g + cc
                last = cc == 3
                tr_done[(j, g)] = P.emit(
                    "pe",
                    lambda e, o=bt[:, cc * 128:(cc + 1) * 128],
                    i=xn_tm[par][:, c * 128:(c + 1) * 128], idn=ident_s:
                    e.transpose(o, in_=i, identity=idn),
                    waits=([("sDVE", xn_done[j]), ("dC", 96)] + bw) if cc == 0 else [],
                    inc="sPE" if last else None,
                    rd=[("A", 24576 + par * 4096 + c * 256,
                         24576 + par * 4096 + c * 256 + 256), ("ident_s", 0, 256)],
                    wr=[(f"pst{(4 * j + g) % 2}", cc * 256, cc * 256 + 256)])
            ov = xn_fm.rearrange("p (c t) -> p c t", c=ND)[:, 4 * g:4 * g + 4,
                                                          j * 128:(j + 1) * 128]
            iv = bt.rearrange("p (c t) -> p c t", c=4)
            ev = P.emit(
                "act", lambda e, a=ov, b=iv: e.copy(out=a, in_=b),
                waits=[("sPE", tr_done[(j, g)])], inc="sACT",
                rd=[(f"pst{(4 * j + g) % 2}", 0, 1024)],
                wr=[("xn_fm", cx * 2048 + j * 256, cx * 2048 + j * 256 + 256)
                    for cx in range(4 * g, 4 * g + 4)])
            tb.release(bi, "sACT", ev)
        last_xn_read[j] = tr_done[(j, 3)]
        xn_tile_ready[j] = P.cnt["sACT"]
    XN_READY = P.cnt["sACT"]
    XN_READY_D = P.cnt["sDVE"]

    P.enabled = "p1" in ph
    ring_free = [[] for _ in range(4)]
    kbf_free = [[], [], [], []]
    qev = [0] * ND
    qev_last = 0

    def qkv_slab(P, si, wb, ec, nq=2):
        slot = si % 4
        span = T // nq
        lw = list(ring_free[slot])
        ring_free[slot] = []
        ld = P.emit(
            "gp", lambda e, a=qkv_ring[slot], b=wb[ec, :, :]:
            e.dma_start(out=a, in_=b), waits=lw, inc=f"dQ{slot}", inc_n=16,
            wr=[("A", 65536 + slot * 4096, 65536 + slot * 4096 + 4096)])
        gdone = [0] * nq
        binfo = []
        for qt in range(nq):
            bi, bt2, bw = fb.acquire()
            last_tile = (qt + 1) * (NTT // nq) - 1
            for kc in range(ND):
                gdone[qt] = P.emit(
                    "pe",
                    lambda e, o=bt2[:, 0:span], l=qkv_ring[slot][:, kc * 128:(kc + 1) * 128],
                    r=xn_fm[:, kc * T + qt * span:kc * T + (qt + 1) * span],
                    st=(kc == 0), sp=(kc == ND - 1):
                    e.matmul(o, lhsT=l, rhs=r, start=st, stop=sp),
                    waits=([(f"dQ{slot}", ld), ("sACT", xn_tile_ready[last_tile]),
                            ("sDVE", XN_READY_D if nq == 2 else 0)] + bw) if kc == 0 else [],
                    inc="sPE" if kc == ND - 1 else None,
                    rd=[("A", 65536 + slot * 4096 + kc * 256,
                         65536 + slot * 4096 + kc * 256 + 256),
                        ("xn_fm", kc * 2048 + qt * span, kc * 2048 + (qt + 1) * span)],
                    wr=[(f"psf{bi}", 0, span * 4)])
            binfo.append((bi, bt2))
        ring_free[slot].append(("sPE", gdone[nq - 1]))
        return gdone, binfo

    NQ = 2
    QS = T // NQ
    for ec in range(ND):
        gdone, binfo = qkv_slab(P, ec * 2, wk_b, ec, nq=NQ)
        kevs = [0] * NQ
        for qt in range(NQ):
            bi, bt2 = binfo[qt]
            kb, ko = (qt, 0) if NQ == 2 else (qt // 2, (qt % 2) * QS)
            kevs[qt] = P.emit(
                "act", lambda e, a=kbf[kb][:, ko:ko + QS], b=bt2[:, 0:QS]:
                e.copy(out=a, in_=b),
                waits=[("sPE", gdone[qt])] + kbf_free[qt], inc="sACT",
                rd=[(f"psf{bi}", 0, QS * 4)], wr=[(f"kbf{kb}", ko * 2, ko * 2 + QS * 2)])
            kbf_free[qt] = []
            fb.release(bi, "sACT", kevs[qt])
        gdone, binfo = qkv_slab(P, ec * 2 + 1, wv_b, ec, nq=NQ)
        for qt in range(NQ):
            bi, bt2 = binfo[qt]
            kb, ko = (qt, 0) if NQ == 2 else (qt // 2, (qt % 2) * QS)
            dst = ec * T + qt * QS
            pd = P.emit(
                "dve", lambda e, a=p_sl[:, dst:dst + QS], b=kbf[kb][:, ko:ko + QS],
                c=bt2[:, 0:QS]:
                e.tensor_mul(a, b, c),
                waits=[("sPE", gdone[qt]), ("sACT", kevs[qt])], inc="sDVE",
                rd=[(f"kbf{kb}", ko * 2, ko * 2 + QS * 2), (f"psf{bi}", 0, QS * 4)],
                wr=[("A", 32768 + dst * 2, 32768 + dst * 2 + QS * 2)])
            kbf_free[qt].append(("sDVE", pd))
            fb.release(bi, "sDVE", pd)
        P.emit("dve", lambda e, a=totals[:, ec:ec + 1], b=p_sl[:, ec * T:(ec + 1) * T]:
               e.reduce_sum(a, b, axis=AX), inc="sDVE",
               rd=[("A", 32768 + ec * 2048, 32768 + ec * 2048 + 2048)],
               wr=[("totals", ec * 4, ec * 4 + 4)])
    TOTALS_DONE = P.cnt["sDVE"]

    P.enabled = ("cc" in ph) or ("cc1" in ph)
    import os as _osg
    _NPRE = int(_osg.environ.get("NPRE", "6"))
    def gu_slot_of(H, s):
        return s if (H == 0 and s < _NPRE) else (s % 3)

    def gu_slot_bytes(slot):
        base = (GUD + GUA)[slot]
        return ("A", base, base + 8192)

    gu_pre = {}

    def emit_gu_pre():
        for s in range(_NPRE):
            wb2 = gate_b if s % 2 == 0 else up_b
            slot = s
            gu_pre[s] = P.emit(
                "gp", lambda e, a=gu_slot[slot], b=wb2[s // 2, :, :]:
                e.dma_start(out=a, in_=b), inc=f"dG{slot}", inc_n=16,
                wr=[gu_slot_bytes(slot)])

    P.enabled = "p1" in ph
    def q_chunk(ec):
        gdone, binfo = qkv_slab(P, 32 + ec, wq_b, ec)
        for h in range(2):
            bi, bt2 = binfo[h]
            dst = ec * T + h * TH
            qev[ec] = P.emit(
                "act", lambda e, a=q_sl[:, dst:dst + TH], b=bt2:
                e.copy(out=a, in_=b),
                waits=[("sPE", gdone[h])], inc="sACT",
                rd=[(f"psf{bi}", 0, 2048)],
                wr=[("A", dst * 2, dst * 2 + 1024)])
            fb.release(bi, "sACT", qev[ec])
    for ec in range(4):
        q_chunk(ec)
    qev_last = P.cnt["sACT"]

    P.enabled = ("cc" in ph) or ("cc2" in ph)
    ccin_ld = P.emit("sp", lambda e, a=cc_in.rearrange("o (p c) -> p o c", p=128),
           b=totals.rearrange("p (o c) -> p o c", o=1): e.dma_start(out=a, in_=b),
           waits=[("sDVE", TOTALS_DONE)], inc="dI", inc_n=16,
           rd=[("totals", 0, 64)], wr=[("Dcc_in", 0, 8192)])

    P.enabled = ("cc" in ph) or ("cc3" in ph)
    def do_cc(e):
        return e.collective_compute(
            "AllGather", ALU.bypass, replica_groups=[list(range(8))],
            ins=[cc_in.opt()], outs=[cc_out.opt()])
    P.emit("gp", do_cc, waits=[("dI", 16)], inc="sCC", inc_n=1,
           rd=[("Dcc_in", 0, 8192)], wr=[("Dcc_out", 0, 65536)])
    P.enabled = ("cc" in ph) or ("cc4" in ph)
    gath_ld = P.emit(
        "sp", lambda e, a=gath.rearrange("p (r c) -> p r c", r=8),
        b=cc_out.rearrange("r (p c) -> p r c", p=128):
        e.dma_start(out=a, in_=b), waits=[("sCC", 1)], inc="dH", inc_n=16,
        rd=[("Dcc_out", 0, 65536)], wr=[("gath", 0, 512)])

    P.enabled = "p1" in ph
    for ec in range(4, ND):
        q_chunk(ec)
    qev_last = P.cnt["sACT"]
    P.enabled = ("cc" in ph) or ("cc1" in ph)
    emit_gu_pre()

    P.enabled = ("cc" in ph) or ("cc5" in ph)
    off_done = P.emit("dve", lambda e, a=off, b=gath[:, 0:16], m=mask_s[:, 0:1]:
           e.tensor_scalar(out=a, in0=b, scalar1=m, scalar2=None, op0=ALU.mult),
           waits=[("dH", 16), ("dC", 96)], inc="sDVE",
           rd=[("gath", 0, 64), ("mask_s", 0, 4)], wr=[("off", 0, 64)])
    for r in range(1, 8):
        off_done = P.emit("dve",
               lambda e, a=off, b=gath[:, r * 16:(r + 1) * 16], m=mask_s[:, r:r + 1]:
               e.scalar_tensor_tensor(out=a, in0=b, scalar=m, in1=a,
                                      op0=ALU.mult, op1=ALU.add),
               waits=[("sDVE", off_done)], inc="sDVE",
               rd=[("gath", r * 64, r * 64 + 64), ("mask_s", r * 4, r * 4 + 4),
                   ("off", 0, 64)], wr=[("off", 0, 64)])

    P.enabled = "scan" in ph
    first_scan_w = [("sDVE", off_done)] if "cc" in ph else []
    for ec in range(ND):
        P.emit("dve",
               lambda e, a=kvt, b=p_sl[:, ec * T:(ec + 1) * T], i=off[:, ec:ec + 1]:
               e.tensor_tensor_scan(out=a, data0=b, data1=b, initial=i,
                                    op0=ALU.add, op1=ALU.bypass),
               waits=first_scan_w, inc="sDVE",
               rd=[("A", 32768 + ec * 2048, 32768 + ec * 2048 + 2048),
                   ("off", ec * 4, ec * 4 + 4)], wr=[("kvt", 0, 4096)])
        P.emit("dve",
               lambda e, a=vhat[:, ec * T:(ec + 1) * T], b=kvt,
               c=q_sl[:, ec * T:(ec + 1) * T]:
               e.scalar_tensor_tensor(out=a, in0=b, scalar=DENOM, in1=c,
                                      op0=ALU.mult, op1=ALU.mult),
               waits=[("sACT", qev[ec])], inc="sDVE",
               rd=[("kvt", 0, 4096), ("A", ec * 2048, ec * 2048 + 2048)],
               wr=[("vhat", ec * 2048, ec * 2048 + 2048)])
    VHAT_DONE = P.cnt["sDVE"]
    if debug:
        dbg = {}
        for nm, ap_, dt in (("d_q", q_sl, BF16), ("d_p", p_sl, BF16),
                            ("d_tot", totals, F32), ("d_off", off, F32),
                            ("d_gath", gath, F32)):
            dd = nc.dram_tensor(nm, [128, ap_.shape[1]], dt, kind="ExternalOutput").ap()
            P.emit("sp", lambda e, a=dd, b=ap_: e.dma_start(out=a, in_=b),
                   waits=[("sDVE", VHAT_DONE)], inc="dB", inc_n=16)

    P.enabled = "p2" in ph
    gate_ev = {}
    up_ev = {}
    down_ev = {}
    slot_free = [[] for _ in range(6)]
    dn_free = [[], []]
    first_down_evac = True
    down_pass_pe = [0, 0]
    gu_pass_pe = [0, 0]
    a_slabs_done = 0
    for H in range(2):
        for s in range(2 * (NI // 2)):
            isl = s // 2
            is_up = s % 2 == 1
            slot = gu_slot_of(H, s)
            if H == 0 and s < _NPRE:
                ld = gu_pre[s]
            else:
                lw = list(slot_free[slot])
                slot_free[slot] = []
                wb2 = up_b if is_up else gate_b
                ld = P.emit("gp", lambda e, a=gu_slot[slot], b=wb2[isl, :, :]:
                            e.dma_start(out=a, in_=b), waits=lw,
                            inc=f"dG{slot}", inc_n=16, wr=[gu_slot_bytes(slot)])
            last_mm = 0
            for jj in range(2):
                ic = 2 * isl + jj
                bi, bt2, bw = fb.acquire()
                gg = 0
                for kc in range(ND):
                    gg = P.emit(
                        "pe",
                        lambda e, o=bt2,
                        l=gu_slot[slot][:, kc * 256 + jj * 128:kc * 256 + jj * 128 + 128],
                        r=xn_fm[:, kc * T + H * TH:kc * T + H * TH + TH],
                        st=(kc == 0), sp=(kc == ND - 1):
                        e.matmul(o, lhsT=l, rhs=r, start=st, stop=sp),
                        waits=([(f"dG{slot}", ld)] + bw) if kc == 0 else [],
                        inc="sPE" if kc == ND - 1 else None,
                        rd=[(gu_slot_bytes(slot)[0],
                             gu_slot_bytes(slot)[1] + kc * 512 + jj * 256,
                             gu_slot_bytes(slot)[1] + kc * 512 + jj * 256 + 256),
                            ("xn_fm", kc * 2048 + H * 1024, kc * 2048 + H * 1024 + 1024)],
                        wr=[(f"psf{bi}", 0, 2048)])
                last_mm = gg
                if not is_up:
                    gate_ev[(H, ic)] = P.emit(
                        "act", lambda e, a=hdn[:, ic * TH:(ic + 1) * TH], b=bt2,
                        s2=gs_s[:, ic:ic + 1]:
                        e.activation(out=a, in_=b, func=AF.Silu, scale=s2),
                        waits=[("sPE", gg), ("dC", 96)], inc="sACT",
                        rd=[(f"psf{bi}", 0, 2048), ("gs_s", ic * 4, ic * 4 + 4)],
                        wr=[("A", 65536 + ic * 1024, 65536 + ic * 1024 + 1024)])
                    fb.release(bi, "sACT", gate_ev[(H, ic)])
                else:
                    up_ev[(H, ic)] = P.emit(
                        "dve", lambda e, a=hdn[:, ic * TH:(ic + 1) * TH], b=bt2,
                        s2=us_s[:, ic:ic + 1]:
                        e.scalar_tensor_tensor(out=a, in0=b, scalar=s2, in1=a,
                                               op0=ALU.mult, op1=ALU.mult),
                        waits=[("sPE", gg), ("dC", 96),
                               ("sACT", gate_ev[(H, ic)])], inc="sDVE",
                        rd=[(f"psf{bi}", 0, 2048), ("us_s", ic * 4, ic * 4 + 4),
                            ("A", 65536 + ic * 1024, 65536 + ic * 1024 + 1024)],
                        wr=[("A", 65536 + ic * 1024, 65536 + ic * 1024 + 1024)])
                    fb.release(bi, "sDVE", up_ev[(H, ic)])
            slot_free[slot].append(("sPE", last_mm))
            if H == 0 and s == 5:
                a_slabs_done = last_mm
        HDN_ACT = gate_ev[(H, NI - 1)]
        HDN_DVE = up_ev[(H, NI - 1)]
        gu_pass_pe[H] = P.cnt["sPE"]
        for dc in range(ND):
            slot = dc % 2
            lw = list(dn_free[slot])
            dn_free[slot] = []
            if H == 0 and dc < 2:
                lw.append(("sDVE", VHAT_DONE))
            ld = P.emit("gp", lambda e, a=dn_ring[slot], b=down_b[dc, :, :]:
                        e.dma_start(out=a, in_=b), waits=lw, inc=f"dD{slot}", inc_n=16,
                        wr=[("A", dn_off[slot], dn_off[slot] + 11264)])
            bi, bt2, bw = fb.acquire()
            dg = 0
            for ic in range(NI):
                dg = P.emit(
                    "pe",
                    lambda e, o=bt2, l=dn_ring[slot][:, ic * 128:(ic + 1) * 128],
                    r=hdn[:, ic * TH:(ic + 1) * TH],
                    st=(ic == 0), sp=(ic == NI - 1):
                    e.matmul(o, lhsT=l, rhs=r, start=st, stop=sp),
                    waits=([(f"dD{slot}", ld), ("sACT", HDN_ACT),
                            ("sDVE", HDN_DVE)] + bw) if ic == 0 else [],
                    inc="sPE" if ic == NI - 1 else None,
                    rd=[("A", dn_off[slot] + ic * 256, dn_off[slot] + ic * 256 + 256),
                        ("A", 65536 + ic * 1024, 65536 + ic * 1024 + 1024)],
                    wr=[(f"psf{bi}", 0, 2048)])
            dn_free[slot].append(("sPE", dg))
            w = [("sPE", dg)]
            if first_down_evac:
                w.append(("sDVE", VHAT_DONE))
                first_down_evac = False
            down_ev[(H, dc)] = P.emit(
                "act", lambda e, a=gffn[:, dc * T + H * TH:dc * T + H * TH + TH],
                b=bt2, s2=ds_s[:, dc:dc + 1]:
                e.activation(out=a, in_=b, func=AF.Silu, scale=s2),
                waits=w + [("dC", 96)], inc="sACT",
                rd=[(f"psf{bi}", 0, 2048), ("ds_s", dc * 4, dc * 4 + 4)],
                wr=[("A", 32768 + (dc * T + H * TH) * 2,
                     32768 + (dc * T + H * TH) * 2 + 1024)])
            fb.release(bi, "sACT", down_ev[(H, dc)])
        down_pass_pe[H] = P.cnt["sPE"]

    P.enabled = "p3" in ph
    wo_free = [[], []]
    store_cnt = [0] * ND
    fin = {}
    for fc in range(ND):
        slot = fc % 2
        lw = list(wo_free[slot])
        wo_free[slot] = []
        lw.append(("sPE", gu_pass_pe[1]))
        ld = P.emit("gp", lambda e, a=wo_ring[slot], b=wo_b[fc, :, :]:
                    e.dma_start(out=a, in_=b), waits=lw, inc=f"dW{slot}", inc_n=16,
                    wr=[("A", 110592 + slot * 4096, 110592 + slot * 4096 + 4096)])
        par = fc % 2
        w = [("sDVE", VHAT_DONE), ("sPE", down_pass_pe[1])]
        if fc >= 2:
            w.append(("sDVE", fin[(fc - 2, 1)]))
        xfl = P.emit("sp", lambda e, a=xf[par], b=x_fm[fc * 128:(fc + 1) * 128, :]:
                     e.dma_start(out=a, in_=b), waits=w, inc=f"dF{par}", inc_n=16,
                     wr=[("A", par * 4096, par * 4096 + 4096)])
        for h in range(2):
            bi, bt2, bw = fb.acquire()
            wg = 0
            for ec in range(ND):
                wg = P.emit(
                    "pe",
                    lambda e, o=bt2, l=wo_ring[slot][:, ec * 128:(ec + 1) * 128],
                    r=vhat[:, ec * T + h * TH:ec * T + h * TH + TH],
                    st=(ec == 0), sp=(ec == ND - 1):
                    e.matmul(o, lhsT=l, rhs=r, start=st, stop=sp),
                    waits=([(f"dW{slot}", ld), ("sDVE", VHAT_DONE)] + bw) if ec == 0 else [],
                    inc="sPE" if ec == ND - 1 else None,
                    rd=[("A", 110592 + slot * 4096 + ec * 256,
                         110592 + slot * 4096 + ec * 256 + 256),
                        ("vhat", ec * 2048 + h * 1024, ec * 2048 + h * 1024 + 1024)],
                    wr=[(f"psf{bi}", 0, 2048)])
            wo_free[slot].append(("sPE", wg))
            P.emit("dve",
                   lambda e, a=tmp_sb[h], b=bt2,
                   g=gffn[:, fc * T + h * TH:fc * T + h * TH + TH]:
                   e.tensor_mul(a, b, g),
                   waits=[("sPE", wg), ("sACT", down_ev[(h, fc)])], inc="sDVE",
                   rd=[(f"psf{bi}", 0, 2048),
                       ("A", 32768 + (fc * T + h * TH) * 2,
                        32768 + (fc * T + h * TH) * 2 + 1024)],
                   wr=[("A", 16384 + h * 2048, 16384 + h * 2048 + 2048)])
            w2 = [(f"dF{par}", xfl)]
            if fc >= 2:
                w2.append((f"dS{par}", store_cnt[fc - 2]))
            fin[(fc, h)] = P.emit(
                "dve", lambda e, a=out_sb[par][:, h * TH:h * TH + TH], b=tmp_sb[h],
                c=xf[par][:, h * TH:h * TH + TH]:
                e.tensor_add(a, b, c), waits=w2, inc="sDVE",
                rd=[("A", 16384 + h * 2048, 16384 + h * 2048 + 2048),
                    ("A", par * 4096 + h * 2048, par * 4096 + h * 2048 + 2048)],
                wr=[("A", 8192 + par * 4096 + h * 2048,
                     8192 + par * 4096 + h * 2048 + 2048)])
            fb.release(bi, "sDVE", fin[(fc, h)])
        store_cnt[fc] = P.emit(
            "sp", lambda e, a=y[fc * 128:(fc + 1) * 128, :], b=out_sb[par]:
            e.dma_start(out=a, in_=b),
            waits=[("sDVE", fin[(fc, 1)])], inc=f"dS{par}", inc_n=16,
            rd=[("A", 8192 + par * 4096, 8192 + par * 4096 + 4096)],
            wr=[("Dy", fc * 4096, fc * 4096 + 4096)])

    P.enabled = True
    if "p3" not in ph:
        P.emit("sp", lambda e: e.dma_start(out=y[0:128, :], in_=out_sb[0]),
               inc="dS0", inc_n=16)
    if debug:
        for nm, ap_, dt, wsem, wval in (
                ("d_xn", xn_fm, BF16, "sACT", XN_READY),
                ("d_vhat", vhat, BF16, "sDVE", VHAT_DONE),
                ("d_gffn", gffn, BF16, "sACT", P.cnt["sACT"]),
                ("d_hdn", hdn, BF16, "sACT", P.cnt["sACT"])):
            dd = nc.dram_tensor(nm, [128, ap_.shape[1]], dt, kind="ExternalOutput").ap()
            P.emit("sp", lambda e, a=dd, b=ap_: e.dma_start(out=a, in_=b),
                   waits=[(wsem, wval)], inc="dB", inc_n=16)
    fw = [("dS0", P.cnt["dS0"]), ("dS1", P.cnt["dS1"])]
    if P.cnt["dB"]:
        fw.append(("dB", P.cnt["dB"]))
    P.emit("sp", None, waits=fw)

    nc._plan = P
    used = set()
    for e in ENGS:
        for ws, fn, inc, inc_n in P.ops[e]:
            used.update(s for s, v in ws)
            if inc:
                used.add(inc)
    from contextlib import ExitStack
    with ExitStack() as ctx:
        sems = {s: ctx.enter_context(nc.semaphore(s)) for s in sorted(used)}
        block = ctx.enter_context(nc.Block())

        def replay(key):
            def run(eng):
                for ws, fn, inc, inc_n in P.ops[key]:
                    for sem, val in ws:
                        eng.wait_ge(sems[sem], val)
                    if fn is None:
                        continue
                    inst = fn(eng)
                    if inc is not None:
                        inst.then_inc(sems[inc], inc_n)
            return run

        block.sync(replay("sp"))
        block.gpsimd(replay("gp"))
        block.tensor(replay("pe"))
        block.scalar(replay("act"))
        block.vector(replay("dve"))

    return nc


def _block_w(WT, K, M, mchunk=128):
    A = WT.reshape(K // 128, 128, M // mchunk, mchunk).transpose(2, 1, 0, 3)
    return np.ascontiguousarray(A.reshape(M // mchunk, 128, (K // 128) * mchunk))


_NC_CACHE = {}


def kernel(L0_attn_x, L0_attn_Wq, L0_attn_Wk, L0_attn_Wv, L0_attn_Wo,
           L0_ffn_gate_w, L0_ffn_up_w, L0_ffn_down_w,
           L0_ffn_gate_s, L0_ffn_up_s, L0_ffn_down_s):
    x = np.asarray(L0_attn_x, dtype=np.float32)
    Wq = np.asarray(L0_attn_Wq, dtype=np.float32)
    Wk = np.asarray(L0_attn_Wk, dtype=np.float32)
    Wv = np.asarray(L0_attn_Wv, dtype=np.float32)
    Wo = np.asarray(L0_attn_Wo, dtype=np.float32)
    gw = np.asarray(L0_ffn_gate_w, dtype=np.float32)
    uw = np.asarray(L0_ffn_up_w, dtype=np.float32)
    dw = np.asarray(L0_ffn_down_w, dtype=np.float32)
    gs = np.asarray(L0_ffn_gate_s, dtype=np.float32)
    us = np.asarray(L0_ffn_up_s, dtype=np.float32)
    ds = np.asarray(L0_ffn_down_s, dtype=np.float32)

    wq_b = _block_w(np.ascontiguousarray(Wq.T), D, D)
    wk_b = _block_w(np.ascontiguousarray(Wk.T), D, D)
    wv_b = _block_w(np.ascontiguousarray(Wv.T), D, D)
    wo_b = _block_w(np.ascontiguousarray(Wo.T), D, D)
    gate_b = _block_w(np.ascontiguousarray(gw.T), D, II, mchunk=256)
    up_b = _block_w(np.ascontiguousarray(uw.T), D, II, mchunk=256)
    down_b = _block_w(np.ascontiguousarray(dw.T), II, D)

    gs_pm = np.ascontiguousarray(gs.reshape(NI, 128).T)
    us_pm = np.ascontiguousarray(us.reshape(NI, 128).T)
    ds_pm = np.ascontiguousarray(ds.reshape(ND, 128).T)
    ident = np.eye(128, dtype=ml_dtypes.bfloat16)

    in_maps = []
    for c in range(8):
        b = c // 4
        t0 = (c % 4) * T
        xs = np.ascontiguousarray(x[b, t0:t0 + T, :])
        xfm = np.ascontiguousarray(xs.T)
        m8 = np.zeros(8, np.float32)
        m8[(c // 4) * 4:c] = 1.0
        mask = np.ascontiguousarray(np.tile(m8, (128, 1)))
        in_maps.append(dict(
            x_tm=xs, x_fm=xfm, wq_b=wq_b, wk_b=wk_b, wv_b=wv_b, wo_b=wo_b,
            gate_b=gate_b, up_b=up_b, down_b=down_b,
            gs_pm=gs_pm, us_pm=us_pm, ds_pm=ds_pm, mask_pm=mask, ident=ident,
            eps_c=np.full((128, 1), EPS, np.float32)))

    if "nc" not in _NC_CACHE:
        _NC_CACHE["nc"] = build_nc()
    nc = _NC_CACHE["nc"]
    res = run_bass_kernel_spmd(nc, in_maps, core_ids=list(range(8)))

    out = np.empty((2, 4096, D), np.float32)
    for c in range(8):
        b = c // 4
        t0 = (c % 4) * T
        out[b, t0:t0 + T, :] = np.asarray(res.results[c]["y"], np.float32).T
    return out


# version 19
# speedup vs baseline: 1.0512x; 1.0264x over previous
import math
import numpy as np
import ml_dtypes

import concourse.bass as bass
import concourse.mybir as mybir
from concourse.bass_utils import run_bass_kernel_spmd

F32 = mybir.dt.float32
BF16 = mybir.dt.bfloat16
U8 = mybir.dt.uint8
AF = mybir.ActivationFunctionType
ALU = mybir.AluOpType
AX = mybir.AxisListType.X

D = 2048
II = 5632
T = 1024
ND = D // 128
NI = II // 128
NTT = T // 128
TH = 512
DENOM = float(np.sqrt(np.float32(2048.0)) / 4096.0)
EPS = 1e-5
INV_D = 1.0 / 2048.0

ENGS = ("sp", "gp", "pe", "act", "dve")


class Prog:

    ENG_SEM = {"dve": "sDVE", "act": "sACT"}

    def __init__(self):
        from collections import defaultdict
        self.ops = {e: [] for e in ENGS}
        self.cnt = defaultdict(int)
        self.last_wait = {}
        self.enabled = True
        self.meta = []
        self.wlog = {e: defaultdict(list) for e in self.ENG_SEM}
        self.rlog = {e: defaultdict(list) for e in self.ENG_SEM}

    def emit(self, eng, fn, waits=(), inc=None, inc_n=1, rd=(), wr=()):
        waits = list(waits)
        if self.enabled and eng in self.ENG_SEM:
            sem = self.ENG_SEM[eng]
            need = 0
            for (buf, lo, hi) in rd:
                for (l2, h2, c2) in self.wlog[eng][buf]:
                    if l2 < hi and lo < h2 and c2 > need:
                        need = c2
            for (buf, lo, hi) in wr:
                for log in (self.wlog[eng][buf], self.rlog[eng][buf]):
                    for (l2, h2, c2) in log:
                        if l2 < hi and lo < h2 and c2 > need:
                            need = c2
            if need > 0:
                waits.append((sem, need))
        ws = []
        for (sem, val) in waits:
            if val is None or val <= 0:
                continue
            k = (eng, sem)
            if self.last_wait.get(k, -1) >= val:
                continue
            self.last_wait[k] = val
            ws.append((sem, val))
        if not self.enabled:
            return self.cnt[inc] if inc is not None else None
        if inc is not None:
            self.cnt[inc] += inc_n
        self.ops[eng].append((ws, fn, inc, inc_n))
        self.meta.append((eng, len(self.ops[eng]) - 1, tuple(ws),
                          (inc, inc_n if inc else 0), tuple(rd), tuple(wr)))
        if eng in self.ENG_SEM and inc is not None:
            c = self.cnt[inc]
            for (buf, lo, hi) in wr:
                self.wlog[eng][buf].append((lo, hi, c))
            for (buf, lo, hi) in rd:
                self.rlog[eng][buf].append((lo, hi, c))
        return self.cnt[inc] if inc is not None else None


class Banks:

    def __init__(self, tensors):
        self.t = tensors
        self.pend = [[] for _ in tensors]
        self.i = 0

    def acquire(self):
        idx = self.i % len(self.t)
        self.i += 1
        w = self.pend[idx]
        self.pend[idx] = []
        return idx, self.t[idx], w

    def release(self, idx, sem, val):
        self.pend[idx].append((sem, val))


def build_nc(ph=("p0", "p1", "cc", "scan", "p2", "p3"), debug=False):
    nc = bass.Bass(trn_type="TRN2", target_bir_lowering=False)

    x_tm = nc.dram_tensor("x_tm", [T, D], F32, kind="ExternalInput").ap()
    x_fm = nc.dram_tensor("x_fm", [D, T], F32, kind="ExternalInput").ap()
    wq_b = nc.dram_tensor("wq_b", [ND, 128, D], F32, kind="ExternalInput").ap()
    wk_b = nc.dram_tensor("wk_b", [ND, 128, D], F32, kind="ExternalInput").ap()
    wv_b = nc.dram_tensor("wv_b", [ND, 128, D], F32, kind="ExternalInput").ap()
    wo_b = nc.dram_tensor("wo_b", [ND, 128, D], F32, kind="ExternalInput").ap()
    gate_b = nc.dram_tensor("gate_b", [NI // 2, 128, 2 * D], F32, kind="ExternalInput").ap()
    up_b = nc.dram_tensor("up_b", [NI // 2, 128, 2 * D], F32, kind="ExternalInput").ap()
    down_b = nc.dram_tensor("down_b", [ND, 128, II], F32, kind="ExternalInput").ap()
    gs_pm = nc.dram_tensor("gs_pm", [128, NI], F32, kind="ExternalInput").ap()
    us_pm = nc.dram_tensor("us_pm", [128, NI], F32, kind="ExternalInput").ap()
    ds_pm = nc.dram_tensor("ds_pm", [128, ND], F32, kind="ExternalInput").ap()
    mask_pm = nc.dram_tensor("mask_pm", [128, 8], F32, kind="ExternalInput").ap()
    ident_d = nc.dram_tensor("ident", [128, 128], BF16, kind="ExternalInput").ap()
    eps_d = nc.dram_tensor("eps_c", [128, 1], F32, kind="ExternalInput").ap()
    y = nc.dram_tensor("y", [D, T], F32, kind="ExternalOutput").ap()

    cc_in_h = nc.dram_tensor("cc_in", [1, 2048], F32)
    cc_out_h = nc.dram_tensor("cc_out", [8, 2048], F32)
    cc_in = cc_in_h.ap()
    cc_out = cc_out_h.ap()

    xn_fm = nc.alloc_sbuf_tensor("xn_fm", [128, ND * T], BF16).ap()
    vhat = nc.alloc_sbuf_tensor("vhat", [128, ND * T], BF16).ap()
    ident_s = nc.alloc_sbuf_tensor("ident_s", [128, 128], BF16).ap()
    gs_s = nc.alloc_sbuf_tensor("gs_s", [128, NI], F32).ap()
    us_s = nc.alloc_sbuf_tensor("us_s", [128, NI], F32).ap()
    ds_s = nc.alloc_sbuf_tensor("ds_s", [128, ND], F32).ap()
    mask_s = nc.alloc_sbuf_tensor("mask_s", [128, 8], F32).ap()
    totals = nc.alloc_sbuf_tensor("totals", [128, 16], F32).ap()
    gath = nc.alloc_sbuf_tensor("gath", [128, 128], F32).ap()
    off = nc.alloc_sbuf_tensor("off", [128, 16], F32).ap()
    kvt = nc.alloc_sbuf_tensor("kvt", [128, T], F32).ap()
    eps_s = nc.alloc_sbuf_tensor("eps_s", [128, 1], F32).ap()
    kbf = [nc.alloc_sbuf_tensor(f"kbf{i}", [128, TH], BF16).ap() for i in range(2)]
    bnst = [nc.alloc_sbuf_tensor(f"bnst{i}", [128, 24], F32).ap() for i in range(2)]
    bnag = [nc.alloc_sbuf_tensor(f"bnag{i}", [128, 2], F32).ap() for i in range(2)]
    st_sum = [nc.alloc_sbuf_tensor(f"st_sum{i}", [128, 1], F32).ap() for i in range(2)]
    st_sq = [nc.alloc_sbuf_tensor(f"st_sq{i}", [128, 1], F32).ap() for i in range(2)]
    st_mean = [nc.alloc_sbuf_tensor(f"st_mean{i}", [128, 1], F32).ap() for i in range(2)]
    st_m2 = [nc.alloc_sbuf_tensor(f"st_m2{i}", [128, 1], F32).ap() for i in range(2)]
    st_var = [nc.alloc_sbuf_tensor(f"st_var{i}", [128, 1], F32).ap() for i in range(2)]
    st_sd = [nc.alloc_sbuf_tensor(f"st_sd{i}", [128, 1], F32).ap() for i in range(2)]
    st_rstd = [nc.alloc_sbuf_tensor(f"st_rstd{i}", [128, 1], F32).ap() for i in range(2)]
    negmr = [nc.alloc_sbuf_tensor(f"negmr{i}", [128, 1], F32).ap() for i in range(2)]

    ARENA = 135168
    arena = nc.alloc_sbuf_tensor("arena", [128, ARENA], U8).ap()

    def carve(off_b, dt, n):
        sz = {F32: 4, BF16: 2}[dt]
        return arena[:, off_b:off_b + n * sz].bitcast(dt)

    xt = [carve(0, F32, D), carve(8192, F32, D)]
    sqtmp = carve(16384, F32, D)
    xn_tm = [carve(24576, BF16, D), carve(28672, BF16, D)]
    q_sl = carve(0, BF16, ND * T)
    xf = [carve(0, F32, T), carve(4096, F32, T)]
    out_sb = [carve(8192, F32, T), carve(12288, F32, T)]
    tmp_sb = [carve(16384, F32, TH), carve(18432, F32, TH)]
    p_sl = carve(32768, BF16, ND * T)
    gffn = carve(32768, BF16, ND * T)
    qkv_ring = [carve(65536 + i * 4096, BF16, D) for i in range(4)]
    hdn = carve(65536, BF16, NI * TH)
    GUD = [110592, 118784, 126976]
    GUA = [81920, 90112, 98304]
    gu_slot = [carve(o, BF16, 2 * D) for o in GUD + GUA]
    dn_off = [0, 5632, 11264, 16896]
    dn_ring = [carve(o, BF16, II // 2) for o in dn_off]
    wo_ring = [carve(110592 + i * 4096, BF16, D) for i in range(2)]

    psf = [nc.alloc_psum_tensor(f"psf{i}", [128, TH], F32).ap() for i in range(6)]
    pst = [nc.alloc_psum_tensor(f"pst{i}", [128, 512], BF16).ap() for i in range(2)]

    P = Prog()
    fb = Banks(psf)
    tb = Banks(pst)

    xl0 = P.emit(
        "sp", lambda e, a=xt[0], b=x_tm[0:128, :]: e.dma_start(out=a, in_=b),
        inc="dX0", inc_n=16, wr=[("A", 0, 8192)])
    P.emit("sp", lambda e, a=ident_s, b=ident_d: e.dma_start(out=a, in_=b),
           inc="dC0", inc_n=16, wr=[("ident_s", 0, 256)])
    P.emit("sp", lambda e, a=eps_s, b=eps_d: e.dma_start(out=a, in_=b),
           inc="dC1", inc_n=16, wr=[("eps_s", 0, 4)])
    for dst, src, nm, sz in ((gs_s, gs_pm, "gs_s", 176), (us_s, us_pm, "us_s", 176),
                             (ds_s, ds_pm, "ds_s", 64), (mask_s, mask_pm, "mask_s", 32)):
        P.emit("sp", lambda e, a=dst, b=src: e.dma_start(out=a, in_=b),
               inc="dC", inc_n=16, wr=[(nm, 0, sz)])

    P.enabled = "p0" in ph
    import os
    _p0n = int(os.environ.get("P0_TILES", "8"))
    xload = [0] * NTT
    xn_tile_ready = [0] * NTT
    xn_done = [0] * NTT
    tr_done = {}
    last_xn_read = [0] * NTT

    for j in range(NTT):
        if j >= _p0n:
            P.enabled = False
        par = j % 2
        w = []
        if j >= 2:
            w.append(("sDVE", xn_done[j - 2]))
        if j == 0:
            xload[0] = xl0
        else:
            xload[j] = P.emit(
                "sp", lambda e, a=xt[par], b=x_tm[j * 128:(j + 1) * 128, :]:
                e.dma_start(out=a, in_=b), waits=w, inc=f"dX{par}", inc_n=16,
                wr=[("A", par * 8192, par * 8192 + 8192)])

        bs_done = 0
        for bq in range(4):
            bs_done = P.emit(
                "dve", lambda e, a=bnst[par][:, bq * 6:(bq + 1) * 6],
                b=xt[par][:, bq * 512:(bq + 1) * 512]:
                e.bn_stats(a, b),
                waits=[(f"dX{par}", xload[j])], inc="sDVE",
                rd=[("A", par * 8192 + bq * 2048, par * 8192 + bq * 2048 + 2048)],
                wr=[(f"bnst{par}", bq * 24, bq * 24 + 24)])
        ag_done = P.emit(
            "dve", lambda e, a=bnag[par], b=bnst[par]: e.bn_aggr(a, b),
            waits=[("sDVE", bs_done)], inc="sDVE",
            rd=[(f"bnst{par}", 0, 96)], wr=[(f"bnag{par}", 0, 8)])
        sd_done = P.emit(
            "act", lambda e, a=st_sd[par], b=bnag[par][:, 1:2], ep=eps_s:
            e.activation(out=a, in_=b, func=AF.Sqrt, bias=ep),
            waits=[("sDVE", ag_done), (f"dX{par}", xload[j]), ("dC1", 16)],
            inc="sACT",
            rd=[(f"bnag{par}", 4, 8), ("eps_s", 0, 4)],
            wr=[(f"st_sd{par}", 0, 4)])
        rstd_done = P.emit("dve", lambda e, a=st_rstd[par], b=st_sd[par]:
               e.reciprocal(a, b), waits=[("sACT", sd_done)], inc="sDVE",
               rd=[(f"st_sd{par}", 0, 4)], wr=[(f"st_rstd{par}", 0, 4)])
        w = [("sPE", last_xn_read[j - 2])] if j >= 2 else []
        xn_done[j] = P.emit(
            "dve", lambda e, a=xn_tm[par], b=xt[par], m=bnag[par][:, 0:1],
            r=st_rstd[par]:
            e.tensor_scalar(out=a, in0=b, scalar1=m, scalar2=r,
                            op0=ALU.subtract, op1=ALU.mult),
            waits=w + [("sDVE", rstd_done)], inc="sDVE",
            rd=[("A", par * 8192, par * 8192 + 8192),
                (f"bnag{par}", 0, 4), (f"st_rstd{par}", 0, 4)],
            wr=[("A", 24576 + par * 4096, 24576 + par * 4096 + 4096)])

        for g in range(4):
            bi, bt, bw = tb.acquire()
            for cc in range(4):
                c = 4 * g + cc
                last = cc == 3
                tr_done[(j, g)] = P.emit(
                    "pe",
                    lambda e, o=bt[:, cc * 128:(cc + 1) * 128],
                    i=xn_tm[par][:, c * 128:(c + 1) * 128], idn=ident_s:
                    e.transpose(o, in_=i, identity=idn),
                    waits=([("sDVE", xn_done[j]), ("dC0", 16)] + bw) if cc == 0 else [],
                    inc="sPE" if last else None,
                    rd=[("A", 24576 + par * 4096 + c * 256,
                         24576 + par * 4096 + c * 256 + 256), ("ident_s", 0, 256)],
                    wr=[(f"pst{(4 * j + g) % 2}", cc * 256, cc * 256 + 256)])
            ov = xn_fm.rearrange("p (c t) -> p c t", c=ND)[:, 4 * g:4 * g + 4,
                                                          j * 128:(j + 1) * 128]
            iv = bt.rearrange("p (c t) -> p c t", c=4)
            ev = P.emit(
                "act", lambda e, a=ov, b=iv: e.copy(out=a, in_=b),
                waits=[("sPE", tr_done[(j, g)])], inc="sACT",
                rd=[(f"pst{(4 * j + g) % 2}", 0, 1024)],
                wr=[("xn_fm", cx * 2048 + j * 256, cx * 2048 + j * 256 + 256)
                    for cx in range(4 * g, 4 * g + 4)])
            tb.release(bi, "sACT", ev)
        last_xn_read[j] = tr_done[(j, 3)]
        xn_tile_ready[j] = P.cnt["sACT"]
    XN_READY = P.cnt["sACT"]
    XN_READY_D = P.cnt["sDVE"]

    P.enabled = "p1" in ph
    ring_free = [[] for _ in range(4)]
    kbf_free = [[], [], [], []]
    qev = [0] * ND
    qev_last = 0

    def qkv_slab(P, si, wb, ec, nq=2):
        slot = si % 4
        span = T // nq
        lw = list(ring_free[slot])
        ring_free[slot] = []
        ld = P.emit(
            "gp", lambda e, a=qkv_ring[slot], b=wb[ec, :, :]:
            e.dma_start(out=a, in_=b), waits=lw, inc=f"dQ{slot}", inc_n=16,
            wr=[("A", 65536 + slot * 4096, 65536 + slot * 4096 + 4096)])
        gdone = [0] * nq
        binfo = []
        for qt in range(nq):
            bi, bt2, bw = fb.acquire()
            last_tile = (qt + 1) * (NTT // nq) - 1
            for kc in range(ND):
                gdone[qt] = P.emit(
                    "pe",
                    lambda e, o=bt2[:, 0:span], l=qkv_ring[slot][:, kc * 128:(kc + 1) * 128],
                    r=xn_fm[:, kc * T + qt * span:kc * T + (qt + 1) * span],
                    st=(kc == 0), sp=(kc == ND - 1):
                    e.matmul(o, lhsT=l, rhs=r, start=st, stop=sp),
                    waits=([(f"dQ{slot}", ld), ("sACT", xn_tile_ready[last_tile]),
                            ("sDVE", XN_READY_D if nq == 2 else 0)] + bw) if kc == 0 else [],
                    inc="sPE" if kc == ND - 1 else None,
                    rd=[("A", 65536 + slot * 4096 + kc * 256,
                         65536 + slot * 4096 + kc * 256 + 256),
                        ("xn_fm", kc * 2048 + qt * span, kc * 2048 + (qt + 1) * span)],
                    wr=[(f"psf{bi}", 0, span * 4)])
            binfo.append((bi, bt2))
        ring_free[slot].append(("sPE", gdone[nq - 1]))
        return gdone, binfo

    NQ = 2
    QS = T // NQ
    for ec in range(ND):
        gdone, binfo = qkv_slab(P, ec * 2, wk_b, ec, nq=NQ)
        kevs = [0] * NQ
        for qt in range(NQ):
            bi, bt2 = binfo[qt]
            kb, ko = (qt, 0) if NQ == 2 else (qt // 2, (qt % 2) * QS)
            kevs[qt] = P.emit(
                "act", lambda e, a=kbf[kb][:, ko:ko + QS], b=bt2[:, 0:QS]:
                e.copy(out=a, in_=b),
                waits=[("sPE", gdone[qt])] + kbf_free[qt], inc="sACT",
                rd=[(f"psf{bi}", 0, QS * 4)], wr=[(f"kbf{kb}", ko * 2, ko * 2 + QS * 2)])
            kbf_free[qt] = []
            fb.release(bi, "sACT", kevs[qt])
        gdone, binfo = qkv_slab(P, ec * 2 + 1, wv_b, ec, nq=NQ)
        for qt in range(NQ):
            bi, bt2 = binfo[qt]
            kb, ko = (qt, 0) if NQ == 2 else (qt // 2, (qt % 2) * QS)
            dst = ec * T + qt * QS
            pd = P.emit(
                "dve", lambda e, a=p_sl[:, dst:dst + QS], b=kbf[kb][:, ko:ko + QS],
                c=bt2[:, 0:QS]:
                e.tensor_mul(a, b, c),
                waits=[("sPE", gdone[qt]), ("sACT", kevs[qt])], inc="sDVE",
                rd=[(f"kbf{kb}", ko * 2, ko * 2 + QS * 2), (f"psf{bi}", 0, QS * 4)],
                wr=[("A", 32768 + dst * 2, 32768 + dst * 2 + QS * 2)])
            kbf_free[qt].append(("sDVE", pd))
            fb.release(bi, "sDVE", pd)
        P.emit("dve", lambda e, a=totals[:, ec:ec + 1], b=p_sl[:, ec * T:(ec + 1) * T]:
               e.reduce_sum(a, b, axis=AX), inc="sDVE",
               rd=[("A", 32768 + ec * 2048, 32768 + ec * 2048 + 2048)],
               wr=[("totals", ec * 4, ec * 4 + 4)])
    TOTALS_DONE = P.cnt["sDVE"]

    P.enabled = ("cc" in ph) or ("cc1" in ph)
    import os as _osg
    _NPRE = int(_osg.environ.get("NPRE", "6"))
    def gu_slot_of(H, s):
        return s if (H == 0 and s < _NPRE) else (s % 3)

    def gu_slot_bytes(slot):
        base = (GUD + GUA)[slot]
        return ("A", base, base + 8192)

    gu_pre = {}

    def emit_gu_pre():
        for s in range(_NPRE):
            wb2 = gate_b if s % 2 == 0 else up_b
            slot = s
            gu_pre[s] = P.emit(
                "gp", lambda e, a=gu_slot[slot], b=wb2[s // 2, :, :]:
                e.dma_start(out=a, in_=b), inc=f"dG{slot}", inc_n=16,
                wr=[gu_slot_bytes(slot)])

    P.enabled = "p1" in ph
    def q_chunk(ec):
        gdone, binfo = qkv_slab(P, 32 + ec, wq_b, ec)
        for h in range(2):
            bi, bt2 = binfo[h]
            dst = ec * T + h * TH
            qev[ec] = P.emit(
                "act", lambda e, a=q_sl[:, dst:dst + TH], b=bt2:
                e.copy(out=a, in_=b),
                waits=[("sPE", gdone[h])], inc="sACT",
                rd=[(f"psf{bi}", 0, 2048)],
                wr=[("A", dst * 2, dst * 2 + 1024)])
            fb.release(bi, "sACT", qev[ec])
    for ec in range(4):
        q_chunk(ec)
    qev_last = P.cnt["sACT"]

    P.enabled = ("cc" in ph) or ("cc2" in ph)
    ccin_ld = P.emit("sp", lambda e, a=cc_in.rearrange("o (p c) -> p o c", p=128),
           b=totals.rearrange("p (o c) -> p o c", o=1): e.dma_start(out=a, in_=b),
           waits=[("sDVE", TOTALS_DONE)], inc="dI", inc_n=16,
           rd=[("totals", 0, 64)], wr=[("Dcc_in", 0, 8192)])

    P.enabled = ("cc" in ph) or ("cc3" in ph)
    def do_cc(e):
        return e.collective_compute(
            "AllGather", ALU.bypass, replica_groups=[list(range(8))],
            ins=[cc_in.opt()], outs=[cc_out.opt()])
    P.emit("gp", do_cc, waits=[("dI", 16)], inc="sCC", inc_n=1,
           rd=[("Dcc_in", 0, 8192)], wr=[("Dcc_out", 0, 65536)])
    P.enabled = ("cc" in ph) or ("cc4" in ph)
    gath_ld = P.emit(
        "sp", lambda e, a=gath.rearrange("p (r c) -> p r c", r=8),
        b=cc_out.rearrange("r (p c) -> p r c", p=128):
        e.dma_start(out=a, in_=b), waits=[("sCC", 1)], inc="dH", inc_n=16,
        rd=[("Dcc_out", 0, 65536)], wr=[("gath", 0, 512)])

    P.enabled = "p1" in ph
    for ec in range(4, ND):
        q_chunk(ec)
    qev_last = P.cnt["sACT"]
    P.enabled = ("cc" in ph) or ("cc1" in ph)
    emit_gu_pre()

    P.enabled = ("cc" in ph) or ("cc5" in ph)
    off_done = P.emit("dve", lambda e, a=off, b=gath[:, 0:16], m=mask_s[:, 0:1]:
           e.tensor_scalar(out=a, in0=b, scalar1=m, scalar2=None, op0=ALU.mult),
           waits=[("dH", 16), ("dC", 64)], inc="sDVE",
           rd=[("gath", 0, 64), ("mask_s", 0, 4)], wr=[("off", 0, 64)])
    for r in range(1, 8):
        off_done = P.emit("dve",
               lambda e, a=off, b=gath[:, r * 16:(r + 1) * 16], m=mask_s[:, r:r + 1]:
               e.scalar_tensor_tensor(out=a, in0=b, scalar=m, in1=a,
                                      op0=ALU.mult, op1=ALU.add),
               waits=[("sDVE", off_done)], inc="sDVE",
               rd=[("gath", r * 64, r * 64 + 64), ("mask_s", r * 4, r * 4 + 4),
                   ("off", 0, 64)], wr=[("off", 0, 64)])

    P.enabled = "scan" in ph
    first_scan_w = [("sDVE", off_done)] if "cc" in ph else []
    for ec in range(ND):
        P.emit("dve",
               lambda e, a=kvt, b=p_sl[:, ec * T:(ec + 1) * T], i=off[:, ec:ec + 1]:
               e.tensor_tensor_scan(out=a, data0=b, data1=b, initial=i,
                                    op0=ALU.add, op1=ALU.bypass),
               waits=first_scan_w, inc="sDVE",
               rd=[("A", 32768 + ec * 2048, 32768 + ec * 2048 + 2048),
                   ("off", ec * 4, ec * 4 + 4)], wr=[("kvt", 0, 4096)])
        P.emit("dve",
               lambda e, a=vhat[:, ec * T:(ec + 1) * T], b=kvt,
               c=q_sl[:, ec * T:(ec + 1) * T]:
               e.scalar_tensor_tensor(out=a, in0=b, scalar=DENOM, in1=c,
                                      op0=ALU.mult, op1=ALU.mult),
               waits=[("sACT", qev[ec])], inc="sDVE",
               rd=[("kvt", 0, 4096), ("A", ec * 2048, ec * 2048 + 2048)],
               wr=[("vhat", ec * 2048, ec * 2048 + 2048)])
    VHAT_DONE = P.cnt["sDVE"]
    if debug:
        dbg = {}
        for nm, ap_, dt in (("d_q", q_sl, BF16), ("d_p", p_sl, BF16),
                            ("d_tot", totals, F32), ("d_off", off, F32),
                            ("d_gath", gath, F32)):
            dd = nc.dram_tensor(nm, [128, ap_.shape[1]], dt, kind="ExternalOutput").ap()
            P.emit("sp", lambda e, a=dd, b=ap_: e.dma_start(out=a, in_=b),
                   waits=[("sDVE", VHAT_DONE)], inc="dB", inc_n=16)

    P.enabled = "p2" in ph
    gate_ev = {}
    up_ev = {}
    down_ev = {}
    slot_free = [[] for _ in range(6)]
    dn_free = [[], [], [], []]
    first_down_evac = True
    down_pass_pe = [0, 0]
    gu_pass_pe = [0, 0]
    a_slabs_done = 0
    for H in range(2):
        for s in range(2 * (NI // 2)):
            isl = s // 2
            is_up = s % 2 == 1
            slot = gu_slot_of(H, s)
            if H == 0 and s < _NPRE:
                ld = gu_pre[s]
            else:
                lw = list(slot_free[slot])
                slot_free[slot] = []
                wb2 = up_b if is_up else gate_b
                ld = P.emit("gp", lambda e, a=gu_slot[slot], b=wb2[isl, :, :]:
                            e.dma_start(out=a, in_=b), waits=lw,
                            inc=f"dG{slot}", inc_n=16, wr=[gu_slot_bytes(slot)])
            last_mm = 0
            for jj in range(2):
                ic = 2 * isl + jj
                bi, bt2, bw = fb.acquire()
                gg = 0
                for kc in range(ND):
                    gg = P.emit(
                        "pe",
                        lambda e, o=bt2,
                        l=gu_slot[slot][:, kc * 256 + jj * 128:kc * 256 + jj * 128 + 128],
                        r=xn_fm[:, kc * T + H * TH:kc * T + H * TH + TH],
                        st=(kc == 0), sp=(kc == ND - 1):
                        e.matmul(o, lhsT=l, rhs=r, start=st, stop=sp),
                        waits=([(f"dG{slot}", ld)] + bw) if kc == 0 else [],
                        inc="sPE" if kc == ND - 1 else None,
                        rd=[(gu_slot_bytes(slot)[0],
                             gu_slot_bytes(slot)[1] + kc * 512 + jj * 256,
                             gu_slot_bytes(slot)[1] + kc * 512 + jj * 256 + 256),
                            ("xn_fm", kc * 2048 + H * 1024, kc * 2048 + H * 1024 + 1024)],
                        wr=[(f"psf{bi}", 0, 2048)])
                last_mm = gg
                if not is_up:
                    gate_ev[(H, ic)] = P.emit(
                        "act", lambda e, a=hdn[:, ic * TH:(ic + 1) * TH], b=bt2,
                        s2=gs_s[:, ic:ic + 1]:
                        e.activation(out=a, in_=b, func=AF.Silu, scale=s2),
                        waits=[("sPE", gg), ("dC", 64)], inc="sACT",
                        rd=[(f"psf{bi}", 0, 2048), ("gs_s", ic * 4, ic * 4 + 4)],
                        wr=[("A", 65536 + ic * 1024, 65536 + ic * 1024 + 1024)])
                    fb.release(bi, "sACT", gate_ev[(H, ic)])
                else:
                    up_ev[(H, ic)] = P.emit(
                        "dve", lambda e, a=hdn[:, ic * TH:(ic + 1) * TH], b=bt2,
                        s2=us_s[:, ic:ic + 1]:
                        e.scalar_tensor_tensor(out=a, in0=b, scalar=s2, in1=a,
                                               op0=ALU.mult, op1=ALU.mult),
                        waits=[("sPE", gg), ("dC", 64),
                               ("sACT", gate_ev[(H, ic)])], inc="sDVE",
                        rd=[(f"psf{bi}", 0, 2048), ("us_s", ic * 4, ic * 4 + 4),
                            ("A", 65536 + ic * 1024, 65536 + ic * 1024 + 1024)],
                        wr=[("A", 65536 + ic * 1024, 65536 + ic * 1024 + 1024)])
                    fb.release(bi, "sDVE", up_ev[(H, ic)])
            slot_free[slot].append(("sPE", last_mm))
            if H == 0 and s == 5:
                a_slabs_done = last_mm
        HDN_ACT = gate_ev[(H, NI - 1)]
        HDN_DVE = up_ev[(H, NI - 1)]
        gu_pass_pe[H] = P.cnt["sPE"]
        NIH = NI // 2
        for dc in range(ND):
            lds = [0, 0]
            slots = [(dc % 2) * 2, (dc % 2) * 2 + 1]
            for hf in range(2):
                slot = slots[hf]
                lw = list(dn_free[slot])
                dn_free[slot] = []
                if H == 0 and dc < 2:
                    lw.append(("sDVE", VHAT_DONE))
                lds[hf] = P.emit(
                    "gp", lambda e, a=dn_ring[slot],
                    b=down_b[dc, :, hf * (II // 2):(hf + 1) * (II // 2)]:
                    e.dma_start(out=a, in_=b), waits=lw, inc=f"dD{slot}", inc_n=16,
                    wr=[("A", dn_off[slot], dn_off[slot] + 5632)])
            bi, bt2, bw = fb.acquire()
            dg = 0
            for ic in range(NI):
                hf = ic // NIH
                slot = slots[hf]
                icl = ic - hf * NIH
                w = []
                if ic == 0:
                    w = [(f"dD{slots[0]}", lds[0]), ("sACT", HDN_ACT),
                         ("sDVE", HDN_DVE)] + bw
                elif ic == NIH:
                    w = [(f"dD{slots[1]}", lds[1])]
                mid_or_last = ic == NIH - 1 or ic == NI - 1
                dg = P.emit(
                    "pe",
                    lambda e, o=bt2, l=dn_ring[slot][:, icl * 128:(icl + 1) * 128],
                    r=hdn[:, ic * TH:(ic + 1) * TH],
                    st=(ic == 0), sp=(ic == NI - 1):
                    e.matmul(o, lhsT=l, rhs=r, start=st, stop=sp),
                    waits=w,
                    inc="sPE" if mid_or_last else None,
                    rd=[("A", dn_off[slot] + icl * 256, dn_off[slot] + icl * 256 + 256),
                        ("A", 65536 + ic * 1024, 65536 + ic * 1024 + 1024)],
                    wr=[(f"psf{bi}", 0, 2048)])
                if ic == NIH - 1:
                    dn_free[slots[0]].append(("sPE", dg))
            dn_free[slots[1]].append(("sPE", dg))
            w = [("sPE", dg)]
            if first_down_evac:
                w.append(("sDVE", VHAT_DONE))
                first_down_evac = False
            down_ev[(H, dc)] = P.emit(
                "act", lambda e, a=gffn[:, dc * T + H * TH:dc * T + H * TH + TH],
                b=bt2, s2=ds_s[:, dc:dc + 1]:
                e.activation(out=a, in_=b, func=AF.Silu, scale=s2),
                waits=w + [("dC", 64)], inc="sACT",
                rd=[(f"psf{bi}", 0, 2048), ("ds_s", dc * 4, dc * 4 + 4)],
                wr=[("A", 32768 + (dc * T + H * TH) * 2,
                     32768 + (dc * T + H * TH) * 2 + 1024)])
            fb.release(bi, "sACT", down_ev[(H, dc)])
        down_pass_pe[H] = P.cnt["sPE"]

    P.enabled = "p3" in ph
    wo_free = [[], []]
    store_cnt = [0] * ND
    fin = {}
    for fc in range(ND):
        slot = fc % 2
        lw = list(wo_free[slot])
        wo_free[slot] = []
        lw.append(("sPE", gu_pass_pe[1]))
        ld = P.emit("gp", lambda e, a=wo_ring[slot], b=wo_b[fc, :, :]:
                    e.dma_start(out=a, in_=b), waits=lw, inc=f"dW{slot}", inc_n=16,
                    wr=[("A", 110592 + slot * 4096, 110592 + slot * 4096 + 4096)])
        par = fc % 2
        w = [("sDVE", VHAT_DONE), ("sPE", down_pass_pe[1])]
        if fc >= 2:
            w.append(("sDVE", fin[(fc - 2, 1)]))
        xfl = P.emit("sp", lambda e, a=xf[par], b=x_fm[fc * 128:(fc + 1) * 128, :]:
                     e.dma_start(out=a, in_=b), waits=w, inc=f"dF{par}", inc_n=16,
                     wr=[("A", par * 4096, par * 4096 + 4096)])
        for h in range(2):
            bi, bt2, bw = fb.acquire()
            wg = 0
            for ec in range(ND):
                wg = P.emit(
                    "pe",
                    lambda e, o=bt2, l=wo_ring[slot][:, ec * 128:(ec + 1) * 128],
                    r=vhat[:, ec * T + h * TH:ec * T + h * TH + TH],
                    st=(ec == 0), sp=(ec == ND - 1):
                    e.matmul(o, lhsT=l, rhs=r, start=st, stop=sp),
                    waits=([(f"dW{slot}", ld), ("sDVE", VHAT_DONE)] + bw) if ec == 0 else [],
                    inc="sPE" if ec == ND - 1 else None,
                    rd=[("A", 110592 + slot * 4096 + ec * 256,
                         110592 + slot * 4096 + ec * 256 + 256),
                        ("vhat", ec * 2048 + h * 1024, ec * 2048 + h * 1024 + 1024)],
                    wr=[(f"psf{bi}", 0, 2048)])
            wo_free[slot].append(("sPE", wg))
            P.emit("dve",
                   lambda e, a=tmp_sb[h], b=bt2,
                   g=gffn[:, fc * T + h * TH:fc * T + h * TH + TH]:
                   e.tensor_mul(a, b, g),
                   waits=[("sPE", wg), ("sACT", down_ev[(h, fc)])], inc="sDVE",
                   rd=[(f"psf{bi}", 0, 2048),
                       ("A", 32768 + (fc * T + h * TH) * 2,
                        32768 + (fc * T + h * TH) * 2 + 1024)],
                   wr=[("A", 16384 + h * 2048, 16384 + h * 2048 + 2048)])
            w2 = [(f"dF{par}", xfl)]
            if fc >= 2:
                w2.append((f"dS{par}", store_cnt[fc - 2]))
            fin[(fc, h)] = P.emit(
                "dve", lambda e, a=out_sb[par][:, h * TH:h * TH + TH], b=tmp_sb[h],
                c=xf[par][:, h * TH:h * TH + TH]:
                e.tensor_add(a, b, c), waits=w2, inc="sDVE",
                rd=[("A", 16384 + h * 2048, 16384 + h * 2048 + 2048),
                    ("A", par * 4096 + h * 2048, par * 4096 + h * 2048 + 2048)],
                wr=[("A", 8192 + par * 4096 + h * 2048,
                     8192 + par * 4096 + h * 2048 + 2048)])
            fb.release(bi, "sDVE", fin[(fc, h)])
        for h2 in range(2):
            store_cnt[fc] = P.emit(
                "sp", lambda e, a=y[fc * 128:(fc + 1) * 128, h2 * TH:(h2 + 1) * TH],
                b=out_sb[par][:, h2 * TH:(h2 + 1) * TH]:
                e.dma_start(out=a, in_=b),
                waits=[("sDVE", fin[(fc, h2)])], inc=f"dS{par}", inc_n=16,
                rd=[("A", 8192 + par * 4096 + h2 * 2048,
                     8192 + par * 4096 + h2 * 2048 + 2048)],
                wr=[("Dy", fc * 4096 + h2 * 2048, fc * 4096 + h2 * 2048 + 2048)])

    P.enabled = True
    if "p3" not in ph:
        P.emit("sp", lambda e: e.dma_start(out=y[0:128, :], in_=out_sb[0]),
               inc="dS0", inc_n=16)
    if debug:
        for nm, ap_, dt, wsem, wval in (
                ("d_xn", xn_fm, BF16, "sACT", XN_READY),
                ("d_vhat", vhat, BF16, "sDVE", VHAT_DONE),
                ("d_gffn", gffn, BF16, "sACT", P.cnt["sACT"]),
                ("d_hdn", hdn, BF16, "sACT", P.cnt["sACT"])):
            dd = nc.dram_tensor(nm, [128, ap_.shape[1]], dt, kind="ExternalOutput").ap()
            P.emit("sp", lambda e, a=dd, b=ap_: e.dma_start(out=a, in_=b),
                   waits=[(wsem, wval)], inc="dB", inc_n=16)
    fw = [("dS0", P.cnt["dS0"]), ("dS1", P.cnt["dS1"])]
    if P.cnt["dB"]:
        fw.append(("dB", P.cnt["dB"]))
    P.emit("sp", None, waits=fw)

    nc._plan = P
    used = set()
    for e in ENGS:
        for ws, fn, inc, inc_n in P.ops[e]:
            used.update(s for s, v in ws)
            if inc:
                used.add(inc)
    from contextlib import ExitStack
    with ExitStack() as ctx:
        sems = {s: ctx.enter_context(nc.semaphore(s)) for s in sorted(used)}
        block = ctx.enter_context(nc.Block())

        def replay(key):
            def run(eng):
                for ws, fn, inc, inc_n in P.ops[key]:
                    for sem, val in ws:
                        eng.wait_ge(sems[sem], val)
                    if fn is None:
                        continue
                    inst = fn(eng)
                    if inc is not None:
                        inst.then_inc(sems[inc], inc_n)
            return run

        block.sync(replay("sp"))
        block.gpsimd(replay("gp"))
        block.tensor(replay("pe"))
        block.scalar(replay("act"))
        block.vector(replay("dve"))

    return nc


def _block_w(WT, K, M, mchunk=128):
    A = WT.reshape(K // 128, 128, M // mchunk, mchunk).transpose(2, 1, 0, 3)
    return np.ascontiguousarray(A.reshape(M // mchunk, 128, (K // 128) * mchunk))


_NC_CACHE = {}


def kernel(L0_attn_x, L0_attn_Wq, L0_attn_Wk, L0_attn_Wv, L0_attn_Wo,
           L0_ffn_gate_w, L0_ffn_up_w, L0_ffn_down_w,
           L0_ffn_gate_s, L0_ffn_up_s, L0_ffn_down_s):
    x = np.asarray(L0_attn_x, dtype=np.float32)
    Wq = np.asarray(L0_attn_Wq, dtype=np.float32)
    Wk = np.asarray(L0_attn_Wk, dtype=np.float32)
    Wv = np.asarray(L0_attn_Wv, dtype=np.float32)
    Wo = np.asarray(L0_attn_Wo, dtype=np.float32)
    gw = np.asarray(L0_ffn_gate_w, dtype=np.float32)
    uw = np.asarray(L0_ffn_up_w, dtype=np.float32)
    dw = np.asarray(L0_ffn_down_w, dtype=np.float32)
    gs = np.asarray(L0_ffn_gate_s, dtype=np.float32)
    us = np.asarray(L0_ffn_up_s, dtype=np.float32)
    ds = np.asarray(L0_ffn_down_s, dtype=np.float32)

    wq_b = _block_w(np.ascontiguousarray(Wq.T), D, D)
    wk_b = _block_w(np.ascontiguousarray(Wk.T), D, D)
    wv_b = _block_w(np.ascontiguousarray(Wv.T), D, D)
    wo_b = _block_w(np.ascontiguousarray(Wo.T), D, D)
    gate_b = _block_w(np.ascontiguousarray(gw.T), D, II, mchunk=256)
    up_b = _block_w(np.ascontiguousarray(uw.T), D, II, mchunk=256)
    down_b = _block_w(np.ascontiguousarray(dw.T), II, D)

    gs_pm = np.ascontiguousarray(gs.reshape(NI, 128).T)
    us_pm = np.ascontiguousarray(us.reshape(NI, 128).T)
    ds_pm = np.ascontiguousarray(ds.reshape(ND, 128).T)
    ident = np.eye(128, dtype=ml_dtypes.bfloat16)

    in_maps = []
    for c in range(8):
        b = c // 4
        t0 = (c % 4) * T
        xs = np.ascontiguousarray(x[b, t0:t0 + T, :])
        xfm = np.ascontiguousarray(xs.T)
        m8 = np.zeros(8, np.float32)
        m8[(c // 4) * 4:c] = 1.0
        mask = np.ascontiguousarray(np.tile(m8, (128, 1)))
        in_maps.append(dict(
            x_tm=xs, x_fm=xfm, wq_b=wq_b, wk_b=wk_b, wv_b=wv_b, wo_b=wo_b,
            gate_b=gate_b, up_b=up_b, down_b=down_b,
            gs_pm=gs_pm, us_pm=us_pm, ds_pm=ds_pm, mask_pm=mask, ident=ident,
            eps_c=np.full((128, 1), EPS, np.float32)))

    if "nc" not in _NC_CACHE:
        _NC_CACHE["nc"] = build_nc()
    nc = _NC_CACHE["nc"]
    res = run_bass_kernel_spmd(nc, in_maps, core_ids=list(range(8)))

    out = np.empty((2, 4096, D), np.float32)
    for c in range(8):
        b = c // 4
        t0 = (c % 4) * T
        out[b, t0:t0 + T, :] = np.asarray(res.results[c]["y"], np.float32).T
    return out
